# version 08
# baseline (speedup 1.0000x reference)
import numpy as np
import concourse.bass as bass
import concourse.mybir as mybir
from concourse.bass_utils import run_bass_kernel_spmd

F32 = mybir.dt.float32
BF16 = mybir.dt.bfloat16
AF = mybir.ActivationFunctionType
ALU = mybir.AluOpType
AX = mybir.AxisListType

D, E, NS, RK, Lseq, H = 1024, 2048, 16, 64, 4096, 4096
DB, EB, HB = D // 128, E // 128, H // 128
EPS = 1e-5
Lc = 256
NCH = Lseq // Lc
MGRP = 4

ENGS = ("pe", "act", "dve", "pool", "sp")


class Prog:

    DSTREAMS = ("dma:wb0", "dma:wb1", "dma:xin", "dma:out", "dma:prep",
                "dma:small", "dma:tap")

    def __init__(self, nc):
        self.nc = nc
        self.scope = None
        self.steps = {e: [] for e in ENGS}
        self.cnt = {e: 0 for e in ENGS}
        self.dmacnt = {st: 0 for st in self.DSTREAMS}
        self.water = {e: {p: 0 for p in ENGS + self.DSTREAMS} for e in ENGS}
        self.sems = {}

    def alloc_sems(self, es):
        for e in ENGS:
            self.sems[e] = es.enter_context(self.nc.semaphore(f"S{e}"))
        for i, st in enumerate(self.DSTREAMS):
            self.sems[st] = es.enter_context(self.nc.semaphore(f"Sd{i}"))

    paranoid = False

    def need(self, eng, prod, val):
        if val <= 0 or self.water[eng][prod] >= val:
            return
        if prod == eng and not self.paranoid:
            return
        self.water[eng][prod] = val
        sem = self.sems[prod]
        self.steps[eng].append(lambda h, sem=sem, val=val: h.wait_ge(sem, val))

    def run(self, eng, fn):
        self.cnt[eng] += 1
        sem = self.sems[eng]

        def emit(h, fn=fn, sem=sem):
            fn(h).then_inc(sem, 1)
        emit._scope = self.scope
        self.steps[eng].append(emit)
        return (eng, self.cnt[eng])

    def dma(self, dst, src, stream="small"):
        st = "dma:" + stream
        sem = self.sems[st]
        prev = self.dmacnt[st]
        if prev:
            self.need("sp", st, prev)
        self.dmacnt[st] = prev + 16

        def emit(h, dst=dst, src=src, sem=sem):
            h.dma_start(dst, src).then_inc(sem, 16)
        emit._scope = self.scope
        self.steps["sp"].append(emit)
        return (st, self.dmacnt[st])

    def emit_all(self, block):
        nc = self.nc
        for eng, attr in (("pe", "tensor"), ("act", "scalar"), ("dve", "vector"),
                          ("pool", "gpsimd"), ("sp", "sync")):
            steps = self.steps[eng]

            def body(h, steps=steps):
                cur = None
                cur_id = None
                for st in steps:
                    sc = getattr(st, "_scope", None)
                    if sc != cur:
                        if cur is not None:
                            nc.leave_named_scope(cur, cur_id, False)
                        if sc is not None:
                            cur_id, _ = nc.enter_named_scope(sc, False)
                        cur = sc
                    st(h)
                if cur is not None:
                    nc.leave_named_scope(cur, cur_id, False)
            getattr(block, attr)(body)


class Dep:

    def __init__(self):
        self.w = {}
        self.r = {}

    def rd(self, p, eng):
        for pe, v in self.w.items():
            p.need(eng, pe, v)

    def wr(self, p, eng):
        for pe, v in self.w.items():
            p.need(eng, pe, v)
        for pe, v in self.r.items():
            p.need(eng, pe, v)

    def did_rd(self, tok):
        eng, v = tok
        self.r[eng] = max(self.r.get(eng, 0), v)

    def did_wr(self, tok):
        eng, v = tok
        self.w[eng] = max(self.w.get(eng, 0), v)


def build(nch=NCH, taps=(), serialize=False, paranoid=False):
    nc = bass.Bass("TRN2", target_bir_lowering=False,
                   detect_race_conditions=paranoid)
    from contextlib import ExitStack
    es = ExitStack()
    M = Mk(nc, es, nch, set(taps))
    M.serialize = serialize
    M.p.paranoid = paranoid
    with es:
        M.declare()
        M.body()
        M.p.emit_all(M.blk)
    return nc, M.tapnames


class Mk:
    def __init__(self, nc, es, nch, taps):
        self.nc = nc
        self.es = es
        self.nch = nch
        self.taps = taps
        self.tapnames = []
        self.p = Prog(nc)
        self.dep = {}

    def D(self, name):
        return self.dep.setdefault(name, Dep())

    serialize = False

    def op(self, eng, fn, reads=(), writes=(), wchk=()):
        p = self.p
        if self.serialize:
            for o in ENGS:
                p.need(eng, o, p.cnt[o])
            for st in p.DSTREAMS:
                p.need(eng, st, p.dmacnt[st])
        for r in reads:
            self.D(r).rd(p, eng)
        for w in wchk:
            self.D(w).wr(p, eng)
        for w in writes:
            self.D(w).wr(p, eng)
        tok = p.run(eng, fn)
        for r in reads:
            self.D(r).did_rd(tok)
        for w in writes:
            self.D(w).did_wr(tok)
        return tok

    def dma(self, dst, src, reads=(), writes=(), stream="small"):
        p = self.p
        if self.serialize:
            for o in ENGS:
                p.need("sp", o, p.cnt[o])
            for st in p.DSTREAMS:
                p.need("sp", st, p.dmacnt[st])
        for r in reads:
            self.D(r).rd(p, "sp")
        for w in writes:
            self.D(w).wr(p, "sp")
        tok = p.dma(dst, src, stream)
        for r in reads:
            self.D(r).did_rd(tok)
        for w in writes:
            self.D(w).did_wr(tok)
        return tok

    def tap(self, name, ap, shape, deps):
        if name not in self.taps:
            return
        t = self.nc.dram_tensor("tap_" + name, shape, ap.dtype,
                                kind="ExternalOutput").ap()
        self.tapnames.append("tap_" + name)
        self.dma(t, ap, reads=deps, writes=(), stream="tap")

    def declare(self):
        nc, es = self.nc, self.es
        self.hbm = {}

        def hin(name, shape):
            self.hbm[name] = nc.dram_tensor(name, shape, F32,
                                            kind="ExternalInput").ap()
        hin("x_t", [D, Lseq])
        self.out_h = nc.dram_tensor("out_t", [D, Lseq], F32,
                                    kind="ExternalOutput").ap()
        self.big = {}
        for nm, (K, Mm) in {"inA": (D, 2 * E), "inB": (D, 2 * E),
                            "poA": (E, D), "poB": (E, D),
                            "W1": (D, H), "W2": (H, D)}.items():
            hin(nm + "_lhsT", [K, Mm])
            kb, mb = K // 128, Mm // 128
            self.big[nm] = dict(K=K, M=Mm, KB=kb, MB=mb,
                                scr=nc.dram_tensor("scr_" + nm,
                                                   [128, kb * Mm], BF16).ap())
        for nm, sh in [("xpA_lhsT", [E, 96]), ("xpB_lhsT", [E, 96]),
                       ("dtpA_lhsT", [RK, E]), ("dtpB_lhsT", [RK, E]),
                       ("Alog_r", [128, 2 * EB * NS]), ("dtpb_r", [128, 2 * EB]),
                       ("cw_r", [128, 2 * EB * 4]), ("cb_r", [128, 2 * EB]),
                       ("Dsk_r", [128, 2 * EB]),
                       ("lnw_r", [128, 2 * DB]), ("lnb_r", [128, 2 * DB]),
                       ("nfw_r", [128, DB]), ("nfb_r", [128, DB]),
                       ("b1_r", [128, HB]), ("b2_r", [128, DB]),
                       ("sc4_r", [4, 1]), ("sc16_r", [16, 1]),
                       ("onesmean_r", [128, 1]), ("onesrow_r", [1, 128]),
                       ("epsc_r", [128, 1]), ("nhalf_r", [128, 1]),
                       ("delta16_r", [16, NS * 128]), ("ident_r", [128, 128]),
                       ("mask1_r", [4, HB]), ("mask2_r", [4, DB])]:
            hin(nm, sh)

        s = {}

        def sb(name, shape, dt):
            s[name] = es.enter_context(nc.sbuf_tensor(name, shape, dt))
        sb("x_res", [128, DB * Lc], F32)
        sb("t_str", [128, DB * Lc], F32)
        sb("xnf", [128, DB * Lc], F32)
        sb("xin_bf", [128, DB * Lc], BF16)
        sb("y_bf", [128, EB * Lc], BF16)
        sb("xcin", [128, 2 * (Lc + 3)], F32)
        sb("tail3", [128, 2 * EB * 3], F32)
        sb("cconv", [128, 2 * Lc], F32)
        sb("sgtmp", [128, 2 * Lc], F32)
        sb("xc_bf", [128, EB * Lc], BF16)
        sb("zs_bf", [128, EB * Lc], BF16)
        sb("dtneg", [128, EB * Lc], F32)
        sb("dtraw_bf", [64, Lc], BF16)
        sb("Bneg_bf", [16, Lc], BF16)
        sb("C_bf", [16, Lc], BF16)
        sb("B_bc", [128, NS * Lc], BF16)
        sb("C_bc", [128, NS * Lc], BF16)
        sb("u_bf", [128, 2 * Lc], BF16)
        sb("dA", [128, NS * Lc], F32)
        sb("dBu", [128, NS * Lc], BF16)
        sb("h_bf", [128, NS * Lc], BF16)
        sb("ys_f", [128, 2 * Lc], F32)
        sb("ytmp", [128, 2 * Lc], F32)
        sb("carry", [128, 2 * EB * NS], F32)
        sb("hg_bf", [128, HB * Lc], BF16)
        sb("wbuf0", [128, MGRP * 8 * 128], BF16)
        sb("wbuf1", [128, MGRP * 8 * 128], BF16)
        sb("xp_sb", [128, 2 * EB * 96], BF16)
        sb("dtp_sb", [64, 2 * EB * 128], BF16)
        sb("Aneg", [128, 2 * EB * NS], F32)
        sb("ndtpb", [128, 2 * EB], F32)
        sb("cwsb", [128, 2 * EB * 4], F32)
        sb("cbc", [128, 2 * EB], F32)
        sb("Dsk", [128, 2 * EB], F32)
        sb("lnw", [128, 2 * DB], F32)
        sb("lnb", [128, 2 * DB], F32)
        sb("nfw", [128, DB], F32)
        sb("nfb", [128, DB], F32)
        sb("b1c", [128, HB], F32)
        sb("b2c", [128, DB], F32)
        sb("onesm", [128, 1], BF16)
        sb("epsc", [128, 1], F32)
        sb("nhalf", [128, 1], F32)
        sb("onesr_f", [1, 128], F32)
        sb("d16", [16, NS * 128], BF16)
        sb("sc4", [4, 1], F32)
        sb("sc16v", [16, 1], F32)
        sb("murow", [1, Lc], F32)
        sb("m2row", [1, Lc], F32)
        sb("vrow", [1, Lc], F32)
        sb("rrow", [1, Lc], F32)
        sb("ident", [128, 128], BF16)
        sb("mask1", [4, HB], F32)
        sb("mask2", [4, DB], F32)
        sb("u_lt", [128, DB * 16], BF16)
        sb("T1sb", [4, D], BF16)
        sb("T2sb", [4, D], BF16)
        sb("coef_bf", [4, Lc], BF16)
        sb("zbt", [4, 8 * 128], BF16)
        sb("g4", [4, 4 * Lc], BF16)
        self.s = s

        raw = {}

        def psd(name, shape):
            raw[name] = es.enter_context(nc.psum_tensor(name, shape, F32))
        psd("ps_a", [128, Lc])
        psd("ps_b", [128, Lc])
        psd("ps_x", [16, 2 * Lc])
        psd("ps_y", [128, 2 * Lc])
        psd("ps_o", [128, DB * Lc])
        ps = {
            "ps_a": raw["ps_a"][:, :],
            "ps_b": raw["ps_b"][:, :],
            "ps_bm": raw["ps_x"][0:16, 0:Lc],
            "ps_cm": raw["ps_x"][0:16, Lc:2 * Lc],
            "ps_s1": raw["ps_x"][0:1, 0:Lc],
            "ps_s2": raw["ps_x"][0:1, Lc:2 * Lc],
            "ps_bc": raw["ps_y"][:, 0:Lc],
            "ps_dt": raw["ps_y"][0:64, Lc:2 * Lc],
            "ps_o": raw["ps_o"][:, :],
        }
        self.ps = ps
        for alias, owner in (("ps_bm", "ps_x"), ("ps_cm", "ps_x"),
                             ("ps_s1", "ps_x"), ("ps_s2", "ps_x"),
                             ("ps_bc", "ps_y"), ("ps_dt", "ps_y")):
            self.dep[alias] = self.D(owner)

        self.p.alloc_sems(es)
        self.blk = es.enter_context(nc.Block())

    def bc8(self, psname):
        assert psname == "ps_bc"
        t = self.ps[psname].tensor
        return bass.AP(t, 0, [[2 * Lc, 128], [0, DB], [1, Lc]])

    def st0(self, ap_tensor, offset, reps, inner):
        return bass.AP(ap_tensor, offset, [[Lc, 128], [0, reps], [1, inner]])

    def body(self):
        self.p.scope = "prep"
        self.prepped = set()
        self.prep()
        for c in range(self.nch):
            self.chunk(c)
        self.p.scope = None
        p = self.p
        for st in p.DSTREAMS:
            v = p.dmacnt[st]
            if v:
                p.steps["sp"].append(
                    lambda h, sem=p.sems[st], v=v: h.wait_ge(sem, v))

    def prep(self):
        s, hbm, p = self.s, self.hbm, self.p
        op, dma = self.op, self.dma

        for src, dst in [("cw_r", "cwsb"), ("cb_r", "cbc"), ("Dsk_r", "Dsk"),
                         ("lnw_r", "lnw"), ("lnb_r", "lnb"), ("nfw_r", "nfw"),
                         ("nfb_r", "nfb"), ("b1_r", "b1c"), ("b2_r", "b2c"),
                         ("sc4_r", "sc4"), ("sc16_r", "sc16v"),
                         ("onesrow_r", "onesr_f"), ("mask1_r", "mask1"),
                         ("epsc_r", "epsc"), ("nhalf_r", "nhalf"),
                         ("mask2_r", "mask2")]:
            dma(s[dst][:, :], hbm[src], writes=(dst,))
        stg = s["dA"]
        dma(stg[:, 0:2 * EB * NS], hbm["Alog_r"], writes=("dA",))
        op("act", lambda h: h.activation(s["Aneg"][:, :], stg[:, 0:2 * EB * NS],
                                         AF.Exp),
           reads=("dA",), writes=("Aneg",))
        dma(stg[:, 0:2 * EB], hbm["dtpb_r"], writes=("dA",))
        op("dve", lambda h: h.tensor_scalar_mul(s["ndtpb"][:, :],
                                                stg[:, 0:2 * EB], -1.0),
           reads=("dA",), writes=("ndtpb",))
        dma(stg[:, 0:1], hbm["onesmean_r"], writes=("dA",))
        op("dve", lambda h: h.tensor_copy(s["onesm"][:, 0:1], stg[:, 0:1]),
           reads=("dA",), writes=("onesm",))
        dma(stg[0:16, 0:NS * 128], hbm["delta16_r"], writes=("dA",))
        op("dve", lambda h: h.tensor_copy(s["d16"][:, :], stg[0:16, 0:NS * 128]),
           reads=("dA",), writes=("d16",))
        dma(stg[:, 0:128], hbm["ident_r"], writes=("dA",))
        op("dve", lambda h: h.tensor_copy(s["ident"][:, :], stg[:, 0:128]),
           reads=("dA",), writes=("ident",))
        for L_, nm in ((0, "xpA_lhsT"), (1, "xpB_lhsT")):
            for k in range(EB):
                src2 = bass.AP(hbm[nm].tensor, k * 128 * 96, [[96, 128], [1, 96]])
                dma(stg[:, 0:96], src2, writes=("dA",))
                op("act", lambda h, L_=L_, k=k: h.activation(
                    s["xp_sb"][:, (L_ * EB + k) * 96:(L_ * EB + k + 1) * 96],
                    stg[:, 0:96], AF.Copy),
                   reads=("dA",), writes=("xp_sb",))
        for L_, nm in ((0, "dtpA_lhsT"), (1, "dtpB_lhsT")):
            dma(stg[0:64, 0:E], bass.AP(hbm[nm].tensor, 0, [[E, 64], [1, E]]),
                writes=("dA",))
            op("act", lambda h, L_=L_: h.activation(
                s["dtp_sb"][0:64, L_ * EB * 128:(L_ + 1) * EB * 128],
                stg[0:64, 0:E], AF.Copy),
               reads=("dA",), writes=("dtp_sb",))
        op("dve", lambda h: h.memset(s["tail3"][:, :], 0.0), writes=("tail3",))
        self.prep_slot = 0
        self.prep_big("inA")

    def prep_big(self, nm):
        if nm in self.prepped:
            return
        self.prepped.add(nm)
        s, hbm, p = self.s, self.hbm, self.p
        op, dma = self.op, self.dma
        stg = s["dA"]
        oldscope = p.scope
        p.scope = "prep"
        owners_a = ["dA"] + ["dA%d" % n for n in range(NS)]
        owners_b = ["dBu"]
        for sl in range(4):
            for nm2, owners in (("pstg%d" % sl, owners_a), ("pcv%d" % sl, owners_b)):
                d = self.D(nm2)
                for ow in owners:
                    o = self.D(ow)
                    for e, v in o.w.items():
                        d.w[e] = max(d.w.get(e, 0), v)
                    for e, v in o.r.items():
                        d.r[e] = max(d.r.get(e, 0), v)
        if True:
            info = self.big[nm]
            lhsT, scr = hbm[nm + "_lhsT"], info["scr"]
            KB, Mm = info["KB"], info["M"]
            kmajor = nm in ("poA", "poB")
            units = []
            if kmajor:
                for k in range(KB):
                    units.append((bass.AP(lhsT.tensor, k * 128 * Mm,
                                          [[Mm, 128], [1, Mm]]),
                                  k * Mm, Mm))
            else:
                mg = 32 // KB
                for grp in range(info["MB"] // mg):
                    for k in range(KB):
                        units.append((bass.AP(lhsT.tensor,
                                              k * 128 * Mm + grp * mg * 128,
                                              [[Mm, 128], [1, mg * 128]]),
                                      grp * (KB * mg * 128) + k * (mg * 128),
                                      mg * 128))
            if kmajor:
                gsz = 2 * Mm
            else:
                gsz = KB * (32 // KB) * 128
            ldstreams = ("wb0", "wb1", "small")
            ststreams = ("prep", "xin", "out")
            cengs = ("act", "dve")
            for src2, dsto, w in units:
                sl = self.prep_slot % 4
                self.prep_slot += 1
                ldsl = stg[:, sl * 1024:sl * 1024 + w]
                cvsl = s["dBu"][:, sl * 1024:sl * 1024 + w]
                dma(ldsl, src2, writes=("pstg%d" % sl,),
                    stream=ldstreams[sl % 3])
                ceng = "act"
                if ceng == "act":
                    op("act", lambda h, ldsl=ldsl, cvsl=cvsl: h.activation(
                        cvsl, ldsl, AF.Copy),
                       reads=("pstg%d" % sl,), writes=("pcv%d" % sl,))
                else:
                    op("dve", lambda h, ldsl=ldsl, cvsl=cvsl: h.tensor_copy(
                        cvsl, ldsl),
                       reads=("pstg%d" % sl,), writes=("pcv%d" % sl,))
                dst = bass.AP(scr.tensor, dsto, [[KB * Mm, 128], [1, w]])
                gdep = "scr_%s_%d" % (nm, dsto // gsz)
                dma(dst, cvsl, reads=("pcv%d" % sl,), writes=(gdep,),
                    stream=ststreams[sl % 3])
        for sl in range(4):
            for nm2 in ("pstg%d" % sl, "pcv%d" % sl):
                d = self.D(nm2)
                for tok_eng, tok_v in list(d.w.items()) + list(d.r.items()):
                    for ow in ["dA", "dBu"] + ["dA%d" % n for n in range(NS)]:
                        self.D(ow).did_rd((tok_eng, tok_v))

    def ln_stats(self, src_name, bf_into="xin_bf"):
        s, ps, op = self.s, self.ps, self.op
        src = s[src_name]
        op("act", lambda h: h.activation(s[bf_into][:, :], src[:, :], AF.Copy),
           reads=(src_name,), writes=(bf_into,))
        op("act", lambda h: h.activation(s["hg_bf"][:, 0:DB * Lc], s[bf_into][:, :],
                                         AF.Square),
           reads=(bf_into,), writes=("hg_bf",))
        for k in range(DB):
            op("pe", lambda h, k=k: h.matmul(
                ps["ps_s1"][0:1, :], s["onesm"][:, 0:1],
                s[bf_into][:, k * Lc:(k + 1) * Lc],
                start=(k == 0), stop=(k == DB - 1)),
               reads=("onesm", bf_into),
               writes=("ps_s1",) if k == DB - 1 else (),
               wchk=("ps_s1",) if k == 0 else ())
        for k in range(DB):
            op("pe", lambda h, k=k: h.matmul(
                ps["ps_s2"][0:1, :], s["onesm"][:, 0:1],
                s["hg_bf"][:, k * Lc:(k + 1) * Lc],
                start=(k == 0), stop=(k == DB - 1)),
               reads=("onesm", "hg_bf"),
               writes=("ps_s2",) if k == DB - 1 else (),
               wchk=("ps_s2",) if k == 0 else ())
        op("dve", lambda h: h.tensor_copy(s["murow"][0:1, :], ps["ps_s1"][0:1, :]),
           reads=("ps_s1",), writes=("murow",))
        op("dve", lambda h: h.tensor_mul(s["vrow"][0:1, :], s["murow"][0:1, :],
                                         s["murow"][0:1, :]),
           reads=("murow",), writes=("vrow",))
        op("dve", lambda h: h.tensor_sub(s["vrow"][0:1, :], ps["ps_s2"][0:1, :],
                                         s["vrow"][0:1, :]),
           reads=("ps_s2", "vrow"), writes=("vrow",))
        op("act", lambda h: h.activation(s["vrow"][0:1, :], s["vrow"][0:1, :],
                                         AF.Ln, bias=s["epsc"][0:1, 0:1]),
           reads=("vrow", "epsc"), writes=("vrow",))
        op("act", lambda h: h.activation(s["rrow"][0:1, :], s["vrow"][0:1, :],
                                         AF.Exp, scale=s["nhalf"][0:1, 0:1]),
           reads=("vrow", "nhalf"), writes=("rrow",))

    def ln_bcast(self, rowname, psname="ps_bc"):
        s, ps, op = self.s, self.ps, self.op
        op("pe", lambda h: h.matmul(ps[psname][:, :], s["onesr_f"][0:1, :],
                                    s[rowname][0:1, :], start=True, stop=True),
           reads=("onesr_f", rowname), writes=(psname,), wchk=(psname,))

    def ln_apply(self, src_name, out_name, affine=None, out_dt_bf=True):
        s, op = self.s, self.op
        src, out = s[src_name], s[out_name]
        self.ln_bcast("murow")
        mu = self.bc8("ps_bc")
        op("dve", lambda h: h.tensor_sub(s["xnf"][:, :], src[:, :], mu),
           reads=(src_name, "ps_bc"), writes=("xnf",))
        self.ln_bcast("rrow")
        r = self.bc8("ps_bc")
        if affine is None:
            op("dve", lambda h: h.tensor_mul(out[:, :], s["xnf"][:, :], r),
               reads=("xnf", "ps_bc"), writes=(out_name,))
        else:
            wnm, bnm, woff = affine
            op("dve", lambda h: h.tensor_mul(s["xnf"][:, :], s["xnf"][:, :], r),
               reads=("xnf", "ps_bc"), writes=("xnf",))
            wap = self.st0w(s[wnm], woff)
            bap = self.st0w(s[bnm], woff)
            op("dve", lambda h: h.scalar_tensor_tensor(
                out[:, :], s["xnf"][:, :], 1.0, s["xnf"][:, :], ALU.mult,
                ALU.add) if False else h.tensor_scalar(
                out[:, :], s["xnf"][:, :], wap, bap, ALU.mult, ALU.add),
               reads=("xnf", wnm, bnm), writes=(out_name,))

    def st0w(self, tile, off):
        raise RuntimeError("unused")

    def chunk(self, c):
        s, p = self.s, self.p
        op, dma = self.op, self.dma
        src = bass.AP(self.hbm["x_t"].tensor, c * Lc,
                      [[Lseq, 128], [128 * Lseq, DB], [1, Lc]])
        dma(s["x_res"][:, :], src, writes=("x_res",), stream="xin")
        self.p.scope = "ln0"
        self.ln_stats("x_res")
        self.ln_apply("x_res", "t_str")
        for L_ in range(2):
            self.mamba_layer(c, L_)
        self.tap("c%d_t2" % c, s["t_str"][:, :], [128, DB * Lc], ("t_str",))
        self.p.scope = "normf"
        self.ln_stats("t_str")
        self.ln_bcast("murow")
        op("dve", lambda h: h.tensor_sub(s["xnf"][:, :], s["t_str"][:, :],
                                         self.bc8("ps_bc")),
           reads=("t_str", "ps_bc"), writes=("xnf",))
        self.ln_bcast("rrow")
        op("dve", lambda h: h.tensor_mul(s["xnf"][:, :], s["xnf"][:, :],
                                         self.bc8("ps_bc")),
           reads=("xnf", "ps_bc"), writes=("xnf",))
        for k in range(DB):
            op("dve", lambda h, k=k: h.tensor_scalar(
                s["xnf"][:, k * Lc:(k + 1) * Lc], s["xnf"][:, k * Lc:(k + 1) * Lc],
                s["nfw"][:, k:k + 1], s["nfb"][:, k:k + 1],
                ALU.mult, ALU.add),
               reads=("xnf", "nfw", "nfb"), writes=("xnf",))
        op("dve", lambda h: h.tensor_add(s["x_res"][:, :], s["x_res"][:, :],
                                         s["xnf"][:, :]),
           reads=("x_res", "xnf"), writes=("x_res",))
        self.ln_stats("x_res")
        self.ln_bcast("murow")
        op("dve", lambda h: h.tensor_sub(s["xnf"][:, :], s["x_res"][:, :],
                                         self.bc8("ps_bc")),
           reads=("x_res", "ps_bc"), writes=("xnf",))
        self.ln_bcast("rrow")
        op("dve", lambda h: h.tensor_mul(s["xin_bf"][:, :], s["xnf"][:, :],
                                         self.bc8("ps_bc")),
           reads=("xnf", "ps_bc"), writes=("xin_bf",))
        if c == 0:
            self.p.scope = "loraprep"
            self.lora_prep()
        self.p.scope = "ffn"
        self.ffn(c)
        dst = bass.AP(self.out_h.tensor, c * Lc,
                      [[Lseq, 128], [128 * Lseq, DB], [1, Lc]])
        dma(dst, s["x_res"][:, :], reads=("x_res",), stream="out")

    def stream_groups(self, nm):
        info = self.big[nm]
        KB = info["KB"]
        mg = 32 // KB
        for grp in range(info["MB"] // mg):
            bufn = "wbuf%d" % (grp % 2)
            buf = self.s[bufn]
            src = bass.AP(info["scr"].tensor, grp * (KB * mg * 128),
                          [[KB * info["M"], 128], [1, KB * mg * 128]])
            self.dma(buf[:, 0:KB * mg * 128], src,
                     reads=("scr_%s_%d" % (nm, grp),), writes=(bufn,),
                     stream="wb0" if grp % 2 == 0 else "wb1")

            def tile(k, mi, buf=buf, mg=mg):
                off = k * (mg * 128) + mi * 128
                return buf[:, off:off + 128]
            yield grp, mg, bufn, tile

    def stream_kgroups(self, nm, kg=2):
        info = self.big[nm]
        KB, MB = info["KB"], info["MB"]
        w = kg * MB * 128
        for grp in range(KB // kg):
            bufn = "wbuf%d" % (grp % 2)
            buf = self.s[bufn]
            src = bass.AP(info["scr"].tensor, grp * w,
                          [[KB * info["M"], 128], [1, w]])
            self.dma(buf[:, 0:w], src, reads=("scr_%s_%d" % (nm, grp),),
                     writes=(bufn,),
                     stream="wb0" if grp % 2 == 0 else "wb1")

            def tile(kig, m, buf=buf, MB=MB):
                off = kig * (MB * 128) + m * 128
                return buf[:, off:off + 128]
            yield grp, bufn, tile

    def mamba_layer(self, c, L_):
        s, ps, p = self.s, self.ps, self.p
        op = self.op
        nm_in = "inA" if L_ == 0 else "inB"
        nm_po = "poA" if L_ == 0 else "poB"
        self.prep_big(nm_in)
        lnoff = L_ * DB
        eoff = L_ * EB

        self.p.scope = "lnin"
        self.ln_stats("t_str")
        self.ln_bcast("murow")
        op("dve", lambda h: h.tensor_sub(s["xnf"][:, :], s["t_str"][:, :],
                                         self.bc8("ps_bc")),
           reads=("t_str", "ps_bc"), writes=("xnf",))
        self.ln_bcast("rrow")
        op("dve", lambda h: h.tensor_mul(s["xnf"][:, :], s["xnf"][:, :],
                                         self.bc8("ps_bc")),
           reads=("xnf", "ps_bc"), writes=("xnf",))
        for k in range(DB):
            op("dve", lambda h, k=k: h.tensor_scalar(
                s["xin_bf"][:, k * Lc:(k + 1) * Lc],
                s["xnf"][:, k * Lc:(k + 1) * Lc],
                s["lnw"][:, lnoff + k:lnoff + k + 1],
                s["lnb"][:, lnoff + k:lnoff + k + 1],
                ALU.mult, ALU.add),
               reads=("xnf", "lnw", "lnb"), writes=("xin_bf",))

        self.tap("c%dL%d_tailpre" % (c, L_), s["tail3"][:, :],
                 [128, 2 * EB * 3], ("tail3",))
        self.p.scope = "inproj"
        for grp, mg, bufn, tile in self.stream_groups(nm_in):
            for mi in range(mg):
                m = grp * mg + mi
                bank = "ps_a" if m % 2 == 0 else "ps_b"
                for k in range(DB):
                    op("pe", lambda h, k=k, mi=mi, bank=bank, tile=tile: h.matmul(
                        ps[bank][:, :], tile(k, mi),
                        s["xin_bf"][:, k * Lc:(k + 1) * Lc],
                        start=(k == 0), stop=(k == DB - 1)),
                       reads=(bufn, "xin_bf"),
                       writes=(bank,) if k == DB - 1 else (),
                       wchk=(bank,) if k == 0 else ())
                self.inproj_evac(c, L_, m, bank, eoff)

        self.tap("c%dL%d_xin" % (c, L_), s["xin_bf"][:, :], [128, DB * Lc],
                 ("xin_bf",))
        self.p.scope = "xproj"
        for k in range(EB):
            xoff = (L_ * EB + k) * 96
            for tgt, dep, c0, c1 in ((ps["ps_a"][0:64, :], "ps_a", 0, 64),
                                     (ps["ps_bm"][0:16, :], "ps_x", 64, 80),
                                     (ps["ps_b"][0:16, :], "ps_b", 80, 96)):
                op("pe", lambda h, k=k, xoff=xoff, tgt=tgt, c0=c0, c1=c1: h.matmul(
                    tgt, s["xp_sb"][:, xoff + c0:xoff + c1],
                    s["xc_bf"][:, k * Lc:(k + 1) * Lc],
                    start=(k == 0), stop=(k == EB - 1)),
                   reads=("xp_sb", "xc_bf"),
                   writes=(dep,) if k == EB - 1 else (),
                   wchk=(dep,) if k == 0 else ())
        op("act", lambda h: h.activation(s["dtraw_bf"][0:64, :], ps["ps_a"][0:64, :],
                                         AF.Copy),
           reads=("ps_a",), writes=("dtraw_bf",))
        op("dve", lambda h: h.tensor_scalar_mul(s["Bneg_bf"][0:16, :],
                                                ps["ps_bm"][0:16, :], -1.0),
           reads=("ps_x",), writes=("Bneg_bf",))
        op("act", lambda h: h.activation(s["C_bf"][0:16, :], ps["ps_b"][0:16, :],
                                         AF.Copy),
           reads=("ps_b",), writes=("C_bf",))
        self.p.scope = "bcbuild"
        for n in range(NS):
            op("pe", lambda h, n=n: h.matmul(
                ps["ps_bc"][:, :], s["d16"][:, n * 128:(n + 1) * 128],
                s["Bneg_bf"][0:16, :], start=True, stop=True),
               reads=("d16", "Bneg_bf"), writes=("ps_bc",), wchk=("ps_bc",))
            op("act", lambda h, n=n: h.activation(
                s["B_bc"][:, n * Lc:(n + 1) * Lc], ps["ps_bc"][:, :], AF.Copy),
               reads=("ps_bc",), writes=("B_bc",))
            op("pe", lambda h, n=n: h.matmul(
                ps["ps_bc"][:, :], s["d16"][:, n * 128:(n + 1) * 128],
                s["C_bf"][0:16, :], start=True, stop=True),
               reads=("d16", "C_bf"), writes=("ps_bc",), wchk=("ps_bc",))
            op("act", lambda h, n=n: h.activation(
                s["C_bc"][:, n * Lc:(n + 1) * Lc], ps["ps_bc"][:, :], AF.Copy),
               reads=("ps_bc",), writes=("C_bc",))

        self.tap("c%dL%d_xc" % (c, L_), s["xc_bf"][:, :], [128, EB * Lc],
                 ("xc_bf",))
        self.tap("c%dL%d_zs" % (c, L_), s["zs_bf"][:, :], [128, EB * Lc],
                 ("zs_bf",))
        self.tap("c%dL%d_Bbc" % (c, L_), s["B_bc"][:, :], [128, NS * Lc],
                 ("B_bc",))
        self.tap("c%dL%d_Cbc" % (c, L_), s["C_bc"][:, :], [128, NS * Lc],
                 ("C_bc",))
        self.p.scope = "dtproj"
        for m in range(EB):
            bank = "ps_a" if m % 2 == 0 else "ps_b"
            op("pe", lambda h, m=m, bank=bank: h.matmul(
                ps[bank][:, :], s["dtp_sb"][0:64, (L_ * EB + m) * 128:
                                            (L_ * EB + m + 1) * 128],
                s["dtraw_bf"][0:64, :], start=True, stop=True),
               reads=("dtp_sb", "dtraw_bf"), writes=(bank,), wchk=(bank,))
            op("act", lambda h, m=m, bank=bank: h.activation(
                s["dtneg"][:, m * Lc:(m + 1) * Lc], ps[bank][:, :], AF.Sigmoid,
                scale=-1.0, bias=s["ndtpb"][:, eoff + m:eoff + m + 1]),
               reads=(bank, "ndtpb"), writes=("dtneg",))
        op("act", lambda h: h.activation(s["dtneg"][:, :], s["dtneg"][:, :], AF.Ln),
           reads=("dtneg",), writes=("dtneg",))

        self.tap("c%dL%d_dtneg" % (c, L_), s["dtneg"][:, :], [128, EB * Lc],
                 ("dtneg",))
        self.p.scope = "scan"
        self.prep_big(nm_po)
        po_iter = self.stream_kgroups(nm_po, kg=2)
        po_cur = None
        for blk in range(EB):
            self.scan_block(c, L_, blk, eoff)
            kig = blk % 2
            if kig == 0:
                po_cur = next(po_iter)
            grp, bufn, tile = po_cur
            for m in range(DB):
                op("pe", lambda h, kig=kig, m=m, tile=tile, blk=blk: h.matmul(
                    ps["ps_o"][:, m * Lc:(m + 1) * Lc], tile(kig, m),
                    s["y_bf"][:, blk * Lc:(blk + 1) * Lc],
                    start=(blk == 0 and m % 2 == 0),
                    stop=(blk == EB - 1 and m % 2 == 1)),
                   reads=(bufn, "y%d" % blk),
                   writes=("ps_o",) if (blk == EB - 1 and m == DB - 1) else (),
                   wchk=("ps_o",) if (blk == 0 and m == 0) else ())
        self.tap("c%dL%d_y" % (c, L_), s["y_bf"][:, :], [128, EB * Lc],
                 tuple("y%d" % b for b in range(EB)))
        self.tap("c%dL%d_carry" % (c, L_), s["carry"][:, :], [128, 2 * EB * NS],
                 ("carry",))

        self.p.scope = "outproj"
        for m in range(DB):
            op("dve", lambda h, m=m: h.tensor_add(
                s["t_str"][:, m * Lc:(m + 1) * Lc],
                s["t_str"][:, m * Lc:(m + 1) * Lc],
                ps["ps_o"][:, m * Lc:(m + 1) * Lc]),
               reads=("t_str", "ps_o"), writes=("t_str",))

    def inproj_evac(self, c, L_, m, bank, eoff):
        s, ps, op = self.s, self.ps, self.op
        if m < EB:
            blk = m
            rot = blk % 2
            xoff = rot * (Lc + 3)
            op("dve", lambda h, blk=blk, xoff=xoff: h.tensor_copy(
                s["xcin"][:, xoff:xoff + 3],
                s["tail3"][:, (L_ * EB + blk) * 3:(L_ * EB + blk + 1) * 3]),
               reads=("tail3",), writes=("xcin%d" % rot,))
            op("dve", lambda h, xoff=xoff, bank=bank: h.tensor_copy(
                s["xcin"][:, xoff + 3:xoff + 3 + Lc], ps[bank][:, :]),
               reads=(bank,), writes=("xcin%d" % rot,))
            op("dve", lambda h, blk=blk, bank=bank: h.tensor_copy(
                s["tail3"][:, (L_ * EB + blk) * 3:(L_ * EB + blk + 1) * 3],
                ps[bank][:, Lc - 3:Lc]),
               reads=(bank,), writes=("tail3",))
            cvo = (L_ * EB + blk) * 4
            op("dve", lambda h, xoff=xoff, rot=rot, cvo=cvo, blk=blk: h.tensor_scalar(
                s["cconv"][:, rot * Lc:(rot + 1) * Lc],
                s["xcin"][:, xoff:xoff + Lc],
                s["cwsb"][:, cvo:cvo + 1], s["cbc"][:, eoff + blk:eoff + blk + 1],
                ALU.mult, ALU.add),
               reads=("xcin%d" % rot, "cwsb", "cbc"), writes=("cconv%d" % rot,))
            for kk in range(1, 4):
                op("dve", lambda h, xoff=xoff, rot=rot, cvo=cvo, kk=kk:
                   h.scalar_tensor_tensor(
                       s["cconv"][:, rot * Lc:(rot + 1) * Lc],
                       s["xcin"][:, xoff + kk:xoff + kk + Lc],
                       s["cwsb"][:, cvo + kk:cvo + kk + 1],
                       s["cconv"][:, rot * Lc:(rot + 1) * Lc],
                       ALU.mult, ALU.add),
                   reads=("xcin%d" % rot, "cwsb", "cconv%d" % rot),
                   writes=("cconv%d" % rot,))
            op("act", lambda h, rot=rot: h.activation(
                s["sgtmp"][:, rot * Lc:(rot + 1) * Lc],
                s["cconv"][:, rot * Lc:(rot + 1) * Lc], AF.Sigmoid),
               reads=("cconv%d" % rot,), writes=("sgtmp%d" % rot,))
            op("dve", lambda h, rot=rot, blk=blk: h.tensor_mul(
                s["xc_bf"][:, blk * Lc:(blk + 1) * Lc],
                s["cconv"][:, rot * Lc:(rot + 1) * Lc],
                s["sgtmp"][:, rot * Lc:(rot + 1) * Lc]),
               reads=("cconv%d" % rot, "sgtmp%d" % rot), writes=("xc_bf",))
        else:
            blk = m - EB
            rot = blk % 2
            op("act", lambda h, rot=rot, bank=bank: h.activation(
                s["sgtmp"][:, rot * Lc:(rot + 1) * Lc], ps[bank][:, :], AF.Sigmoid),
               reads=(bank,), writes=("sgtmp%d" % rot,))
            op("dve", lambda h, rot=rot, bank=bank, blk=blk: h.tensor_mul(
                s["zs_bf"][:, blk * Lc:(blk + 1) * Lc], ps[bank][:, :],
                s["sgtmp"][:, rot * Lc:(rot + 1) * Lc]),
               reads=(bank, "sgtmp%d" % rot), writes=("zs_bf",))

    def scan_block(self, c, L_, blk, eoff):
        s, op = self.s, self.op
        rot = blk % 2
        aoff = (L_ * EB + blk) * NS
        if rot == 0:
            dA = s["dA"][:, :]
            dAdep = "dA"
        else:
            dA = s["hg_bf"][:, :].bitcast(F32)[:, 0:NS * Lc]
            dAdep = "hgA"
            base = self.D("hg_bf")
            for n in range(NS):
                d = Dep()
                d.w = dict(base.w)
                d.r = dict(base.r)
                self.dep["hgA%d" % n] = d
        for n in range(NS):
            op("act", lambda h, n=n, blk=blk, aoff=aoff, dA=dA: h.activation(
                dA[:, n * Lc:(n + 1) * Lc],
                s["dtneg"][:, blk * Lc:(blk + 1) * Lc], AF.Exp,
                scale=s["Aneg"][:, aoff + n:aoff + n + 1]),
               reads=("dtneg", "Aneg"), writes=("%s%d" % (dAdep, n),))
        op("dve", lambda h, blk=blk, rot=rot: h.tensor_mul(
            s["u_bf"][:, rot * Lc:(rot + 1) * Lc],
            s["dtneg"][:, blk * Lc:(blk + 1) * Lc],
            s["xc_bf"][:, blk * Lc:(blk + 1) * Lc]),
           reads=("dtneg", "xc_bf"), writes=("u%d" % rot,))
        u0 = bass.AP(s["u_bf"][:, :].tensor, rot * Lc,
                     [[2 * Lc, 128], [0, NS], [1, Lc]])
        import os as _os2
        dbu_eng = "pool" if _os2.environ.get("DBU_POOL", "0") == "1" else "dve"
        op(dbu_eng, lambda h, u0=u0: h.tensor_mul(s["dBu"][:, :], u0, s["B_bc"][:, :]),
           reads=("u%d" % rot, "B_bc"), writes=("dBu",))
        for n in range(NS):
            if c == 0:
                init = 0.0
            else:
                init = s["carry"][:, (aoff + n):(aoff + n) + 1]
            op("dve", lambda h, n=n, init=init, dA=dA: h.tensor_tensor_scan(
                s["h_bf"][:, n * Lc:(n + 1) * Lc],
                dA[:, n * Lc:(n + 1) * Lc],
                s["dBu"][:, n * Lc:(n + 1) * Lc],
                init, ALU.mult, ALU.add),
               reads=("%s%d" % (dAdep, n), "dBu") + (("carry",) if c > 0 else ()),
               writes=("h_bf",))
        import os as _os
        ymul_eng = "pool" if _os.environ.get("YMUL_POOL", "0") == "1" else "dve"
        op(ymul_eng, lambda h: h.tensor_mul(s["dBu"][:, :], s["h_bf"][:, :],
                                            s["C_bc"][:, :]),
           reads=("h_bf", "C_bc"), writes=("dBu",))
        hv = bass.AP(s["dBu"][:, :].tensor, 0, [[NS * Lc, 128], [1, Lc], [Lc, NS]])
        op("dve", lambda h, hv=hv, rot=rot: h.tensor_reduce(
            s["ys_f"][:, rot * Lc:(rot + 1) * Lc], hv, AX.X, ALU.add),
           reads=("dBu",), writes=("ys%d" % rot,))
        hlast = bass.AP(s["h_bf"][:, :].tensor, Lc - 1, [[NS * Lc, 128], [Lc, NS]])
        op("dve", lambda h, hlast=hlast, aoff=aoff: h.tensor_copy(
            s["carry"][:, aoff:aoff + NS], hlast),
           reads=("h_bf",), writes=("carry",))
        op("dve", lambda h, rot=rot, blk=blk: h.scalar_tensor_tensor(
            s["ytmp"][:, rot * Lc:(rot + 1) * Lc],
            s["xc_bf"][:, blk * Lc:(blk + 1) * Lc],
            s["Dsk"][:, eoff + blk:eoff + blk + 1],
            s["ys_f"][:, rot * Lc:(rot + 1) * Lc], ALU.mult, ALU.add),
           reads=("xc_bf", "Dsk", "ys%d" % rot), writes=("yt%d" % rot,))
        op("dve", lambda h, rot=rot, blk=blk: h.tensor_mul(
            s["y_bf"][:, blk * Lc:(blk + 1) * Lc],
            s["ytmp"][:, rot * Lc:(rot + 1) * Lc],
            s["zs_bf"][:, blk * Lc:(blk + 1) * Lc]),
           reads=("yt%d" % rot, "zs_bf"), writes=("y%d" % blk,))
        if rot == 1:
            base = self.D("hg_bf")
            for n in range(NS):
                d = self.dep["hgA%d" % n]
                for e, v in d.w.items():
                    base.w[e] = max(base.w.get(e, 0), v)
                for e, v in d.r.items():
                    base.r[e] = max(base.r.get(e, 0), v)

    def lora_prep(self):
        s, ps, op = self.s, self.ps, self.op
        src = bass.AP(s["xin_bf"][:, :].tensor, 1, [[DB * Lc, 128], [Lc, DB], [1, 16]])
        op("dve", lambda h, src=src: h.tensor_copy(s["u_lt"][:, :], src),
           reads=("xin_bf",), writes=("u_lt",))
        for tgt, base in (("T1sb", 0), ("T2sb", 8)):
            for blk in range(DB):
                inap = bass.AP(s["u_lt"][:, :].tensor, blk * 16 + base,
                               [[DB * 16, 128], [2, 4]])
                trout = ps["ps_bm"][0:4, 0:64].bitcast(BF16)
                op("pe", lambda h, inap=inap, trout=trout: h.transpose(
                    trout, inap, s["ident"][:, :]),
                   reads=("u_lt", "ident"), writes=("ps_bm",), wchk=("ps_bm",))
                op("act", lambda h, tgt=tgt, blk=blk, trout=trout: h.activation(
                    s[tgt][0:4, blk * 128:(blk + 1) * 128], trout, AF.Copy),
                   reads=("ps_bm",), writes=(tgt,))


    def ffn(self, c):
        s, ps, op = self.s, self.ps, self.op
        self.prep_big("W1")
        self.prep_big("W2")
        for k in range(DB):
            st = bass.AP(s["u_lt"][:, :].tensor, k * 16 + 1, [[DB * 16, 128], [2, 4]])
            op("pe", lambda h, k=k, st=st: h.matmul(
                ps["ps_bm"][0:4, :], st, s["xin_bf"][:, k * Lc:(k + 1) * Lc],
                start=(k == 0), stop=(k == DB - 1)),
               reads=("u_lt", "xin_bf"), writes=("ps_bm",) if k == DB - 1 else (),
               wchk=("ps_bm",) if k == 0 else ())
        op("dve", lambda h: h.tensor_scalar_mul(s["coef_bf"][0:4, :],
                                                ps["ps_bm"][0:4, :],
                                                s["sc4"][0:4, 0:1]),
           reads=("ps_bm", "sc4"), writes=("coef_bf",))
        for grp, mg, bufn, tile in self.stream_groups("W1"):
            for mi in range(mg):
                m = grp * mg + mi
                bank = "ps_a" if m % 2 == 0 else "ps_b"
                for k in range(DB):
                    op("pe", lambda h, k=k, mi=mi, bank=bank, tile=tile: h.matmul(
                        ps[bank][:, :], tile(k, mi),
                        s["xin_bf"][:, k * Lc:(k + 1) * Lc],
                        start=(k == 0), stop=False),
                       reads=(bufn, "xin_bf"),
                       wchk=(bank,) if k == 0 else ())
                zr = (m % 2) * 512
                op("dve", lambda h, m=m, zr=zr: h.tensor_scalar_mul(
                    s["zbt"][0:4, zr:zr + 128],
                    s["T1sb"][0:4, (m % 8) * 128:(m % 8 + 1) * 128],
                    s["mask1"][0:4, m:m + 1]),
                   reads=("T1sb", "mask1"), writes=("zbt%d" % (m % 2),))
                op("pe", lambda h, m=m, bank=bank, zr=zr: h.matmul(
                    ps[bank][:, :], s["zbt"][0:4, zr:zr + 128],
                    s["coef_bf"][0:4, :], start=False, stop=True),
                   reads=("zbt%d" % (m % 2), "coef_bf"), writes=(bank,))
                op("act", lambda h, m=m, bank=bank: h.activation(
                    s["hg_bf"][:, m * Lc:(m + 1) * Lc], ps[bank][:, :], AF.Gelu,
                    bias=s["b1c"][:, m:m + 1]),
                   reads=(bank, "b1c"), writes=("hg_bf",))
        gbanks = ("ps_bm", "ps_cm", "ps_bc", "ps_dt")
        for sp_ in range(4):
            psn = gbanks[sp_]
            for i, kb in enumerate(range(sp_ * 8, sp_ * 8 + 8)):
                st = bass.AP(s["u_lt"][:, :].tensor, (kb % 8) * 16 + 9,
                             [[DB * 16, 128], [2, 4]])
                op("pe", lambda h, kb=kb, st=st, psn=psn, i=i: h.matmul(
                    self.ps[psn][0:4, :],
                    st, s["hg_bf"][:, kb * Lc:(kb + 1) * Lc],
                    start=(i == 0), stop=(i == 7)),
                   reads=("u_lt", "hg_bf"), writes=(psn,) if i == 7 else (),
                   wchk=(psn,) if i == 0 else ())
            op("dve", lambda h, sp_=sp_, psn=psn: h.tensor_scalar_mul(
                s["g4"][0:4, sp_ * Lc:(sp_ + 1) * Lc],
                self.ps[psn][0:4, :],
                s["sc4"][0:4, 0:1]),
               reads=(psn, "sc4"), writes=("g4",))
        for grp, mg, bufn, tile in self.stream_groups("W2"):
            for mi in range(mg):
                m = grp * mg + mi
                bank = "ps_a" if m % 2 == 0 else "ps_b"
                for k in range(HB):
                    op("pe", lambda h, k=k, mi=mi, bank=bank, tile=tile: h.matmul(
                        ps[bank][:, :], tile(k, mi),
                        s["hg_bf"][:, k * Lc:(k + 1) * Lc],
                        start=(k == 0), stop=False),
                       reads=(bufn, "hg_bf"),
                       wchk=(bank,) if k == 0 else ())
                zr = (m % 2) * 512
                for sp_ in range(4):
                    gat = bass.AP(s["T2sb"][:, :].tensor, (m % 2) * 512 + sp_,
                                  [[D, 4], [4, 128]])
                    op("dve", lambda h, m=m, sp_=sp_, gat=gat, zr=zr:
                       h.tensor_scalar_mul(
                           s["zbt"][0:4, zr + sp_ * 128:zr + (sp_ + 1) * 128],
                           gat, s["mask2"][0:4, m:m + 1]),
                       reads=("T2sb", "mask2"), writes=("zbt%d" % (m % 2),))
                for sp_ in range(4):
                    op("pe", lambda h, m=m, sp_=sp_, bank=bank, zr=zr: h.matmul(
                        ps[bank][:, :],
                        s["zbt"][0:4, zr + sp_ * 128:zr + (sp_ + 1) * 128],
                        s["g4"][0:4, sp_ * Lc:(sp_ + 1) * Lc],
                        start=False, stop=(sp_ == 3)),
                       reads=("zbt%d" % (m % 2), "g4"),
                       writes=(bank,) if sp_ == 3 else ())
                op("dve", lambda h, m=m, bank=bank: h.scalar_tensor_tensor(
                    s["x_res"][:, m * Lc:(m + 1) * Lc], ps[bank][:, :],
                    s["b2c"][:, m:m + 1],
                    s["x_res"][:, m * Lc:(m + 1) * Lc], ALU.add, ALU.add),
                   reads=(bank, "b2c", "x_res"), writes=("x_res",))


def make_inputs(inp):
    f = np.float32
    A = np.ascontiguousarray
    outs = []
    const = {
        "sc4_r": np.repeat(inp["L0_ffn_scale"].reshape(1, 1), 4, 0).astype(f),
        "sc16_r": np.repeat(inp["L0_ffn_scale"].reshape(1, 1), 16, 0).astype(f),
        "onesmean_r": np.full((128, 1), 1.0 / D, f),
        "epsc_r": np.full((128, 1), EPS, f),
        "nhalf_r": np.full((128, 1), -0.5, f),
        "onesrow_r": np.ones((1, 128), f),
        "ident_r": np.eye(128, dtype=f),
    }
    d16 = np.zeros((16, NS * 128), f)
    for n in range(NS):
        d16[n, n * 128:(n + 1) * 128] = 1.0
    const["delta16_r"] = d16
    m1 = np.zeros((4, HB), f)
    for b2 in range(HB):
        m1[b2 // 8, b2] = 1.0
    const["mask1_r"] = m1
    m2 = np.zeros((4, DB), f)
    for b in range(DB):
        m2[b // 2, b] = 1.0
    const["mask2_r"] = m2

    def r3(x, nb):
        return A(x.reshape(2, nb, 128).transpose(2, 0, 1).reshape(128, 2 * nb))

    def r1(x, nb):
        return A(x.reshape(nb, 128).T)

    w = {
        "inA_lhsT": A(inp["L0_attn_in_proj_w"][0].T),
        "inB_lhsT": A(inp["L0_attn_in_proj_w"][1].T),
        "poA_lhsT": A(inp["L0_attn_out_proj_w"][0].T),
        "poB_lhsT": A(inp["L0_attn_out_proj_w"][1].T),
        "W1_lhsT": A(inp["L0_ffn_W1"].T),
        "W2_lhsT": A(inp["L0_ffn_W2"].T),
        "xpA_lhsT": A(inp["L0_attn_x_proj_w"][0].T),
        "xpB_lhsT": A(inp["L0_attn_x_proj_w"][1].T),
        "dtpA_lhsT": A(inp["L0_attn_dt_proj_w"][0].T),
        "dtpB_lhsT": A(inp["L0_attn_dt_proj_w"][1].T),
        "Alog_r": A(inp["L0_attn_A_log"].reshape(2, EB, 128, NS)
                    .transpose(2, 0, 1, 3).reshape(128, 2 * EB * NS)),
        "dtpb_r": r3(inp["L0_attn_dt_proj_b"], EB),
        "cw_r": A(inp["L0_attn_conv_w"].reshape(2, EB, 128, 4)
                  .transpose(2, 0, 1, 3).reshape(128, 2 * EB * 4)),
        "cb_r": r3(inp["L0_attn_conv_b"], EB),
        "Dsk_r": r3(inp["L0_attn_D_skip"], EB),
        "lnw_r": r3(inp["L0_attn_ln_w"], DB),
        "lnb_r": r3(inp["L0_attn_ln_b"], DB),
        "nfw_r": r1(inp["L0_attn_norm_f_w"], DB),
        "nfb_r": r1(inp["L0_attn_norm_f_b"], DB),
        "b1_r": r1(inp["L0_ffn_b1"], HB),
        "b2_r": r1(inp["L0_ffn_b2"], DB),
    }
    w.update(const)
    x = inp["L0_attn_x"]
    in_maps = []
    for b in range(x.shape[0]):
        m = dict(w)
        m["x_t"] = A(x[b].T)
        in_maps.append(m)
    return in_maps


def kernel(**inputs):
    in_maps = make_inputs(inputs)
    nc, _ = build()
    res = run_bass_kernel_spmd(nc, in_maps, core_ids=list(range(len(in_maps))))
    return np.stack([np.ascontiguousarray(r["out_t"].T) for r in res.results])


# version 09
# speedup vs baseline: 1.0796x; 1.0796x over previous
import numpy as np
import concourse.bass as bass
import concourse.mybir as mybir
from concourse.bass_utils import run_bass_kernel_spmd
from concourse import dve_ops as _dve_ops
from concourse import dve_spec as _dve_spec
from concourse.dve_op import DveOp as _DveOp


def _mk_mulscan():
    name = "ANT_K_MULSCAN"
    for o in _dve_ops.OPS:
        if o.name == name:
            return o
    spec = _dve_spec.Spec(body=_dve_spec.scan(
        _dve_spec.AluOp.ADD, _dve_spec.Src0 * _dve_spec.Src1))
    shas = {}
    for ver in ("v3", "v4"):
        try:
            shas[ver] = _dve_spec.lower(spec, ver=ver, perf_en=False,
                                        name=name).sha(ver)
        except Exception:
            pass

    def _ref(in0, in1, c0, c1, c2):
        dt_ = np.float64 if in0.dtype == np.float64 else np.float32
        return np.cumsum(in0.astype(dt_) * in1.astype(dt_), axis=1)

    op = _DveOp(name, spec, subdim=False, uops_sha=shas, reference=_ref)
    _dve_ops.OPS.append(op)
    _dve_ops.CUSTOM_DVE_SPECS[name] = spec
    _dve_ops.CUSTOM_DVE_REFS[name] = _ref
    _dve_ops._SUB_OPCODE_FOR_NAME[name] = (
        _dve_ops._CUSTOM_DVE_ROW_BASE + len(_dve_ops.OPS) - 1)
    return op


MULSCAN = _mk_mulscan()

F32 = mybir.dt.float32
BF16 = mybir.dt.bfloat16
AF = mybir.ActivationFunctionType
ALU = mybir.AluOpType
AX = mybir.AxisListType

D, E, NS, RK, Lseq, H = 1024, 2048, 16, 64, 4096, 4096
DB, EB, HB = D // 128, E // 128, H // 128
EPS = 1e-5
Lc = 256
NCH = Lseq // Lc
MGRP = 4

ENGS = ("pe", "act", "dve", "pool", "sp")


class Prog:

    DSTREAMS = ("dma:wb0", "dma:wb1", "dma:xin", "dma:out", "dma:prep",
                "dma:small", "dma:tap")

    def __init__(self, nc):
        self.nc = nc
        self.scope = None
        self.steps = {e: [] for e in ENGS}
        self.cnt = {e: 0 for e in ENGS}
        self.dmacnt = {st: 0 for st in self.DSTREAMS}
        self.water = {e: {p: 0 for p in ENGS + self.DSTREAMS} for e in ENGS}
        self.sems = {}

    def alloc_sems(self, es):
        for e in ENGS:
            self.sems[e] = es.enter_context(self.nc.semaphore(f"S{e}"))
        for i, st in enumerate(self.DSTREAMS):
            self.sems[st] = es.enter_context(self.nc.semaphore(f"Sd{i}"))

    paranoid = False

    def need(self, eng, prod, val):
        if val <= 0 or self.water[eng][prod] >= val:
            return
        if prod == eng and not self.paranoid:
            return
        self.water[eng][prod] = val
        sem = self.sems[prod]
        self.steps[eng].append(lambda h, sem=sem, val=val: h.wait_ge(sem, val))

    def run(self, eng, fn):
        self.cnt[eng] += 1
        sem = self.sems[eng]

        def emit(h, fn=fn, sem=sem):
            fn(h).then_inc(sem, 1)
        emit._scope = self.scope
        self.steps[eng].append(emit)
        return (eng, self.cnt[eng])

    def dma(self, dst, src, stream="small", queue="sp"):
        st = "dma:" + stream
        sem = self.sems[st]
        prev = self.dmacnt[st]
        if prev:
            self.need(queue, st, prev)
        self.dmacnt[st] = prev + 16

        def emit(h, dst=dst, src=src, sem=sem):
            h.dma_start(dst, src).then_inc(sem, 16)
        emit._scope = self.scope
        self.steps[queue].append(emit)
        return (st, self.dmacnt[st])

    def emit_all(self, block):
        nc = self.nc
        for eng, attr in (("pe", "tensor"), ("act", "scalar"), ("dve", "vector"),
                          ("pool", "gpsimd"), ("sp", "sync")):
            steps = self.steps[eng]

            def body(h, steps=steps):
                cur = None
                cur_id = None
                for st in steps:
                    sc = getattr(st, "_scope", None)
                    if sc != cur:
                        if cur is not None:
                            nc.leave_named_scope(cur, cur_id, False)
                        if sc is not None:
                            cur_id, _ = nc.enter_named_scope(sc, False)
                        cur = sc
                    st(h)
                if cur is not None:
                    nc.leave_named_scope(cur, cur_id, False)
            getattr(block, attr)(body)


class Dep:

    def __init__(self):
        self.w = {}
        self.r = {}

    def rd(self, p, eng):
        for pe, v in self.w.items():
            p.need(eng, pe, v)

    def wr(self, p, eng):
        for pe, v in self.w.items():
            p.need(eng, pe, v)
        for pe, v in self.r.items():
            p.need(eng, pe, v)

    def did_rd(self, tok):
        eng, v = tok
        self.r[eng] = max(self.r.get(eng, 0), v)

    def did_wr(self, tok):
        eng, v = tok
        self.w[eng] = max(self.w.get(eng, 0), v)


def build(nch=NCH, taps=(), serialize=False, paranoid=False):
    nc = bass.Bass("TRN2", target_bir_lowering=False,
                   detect_race_conditions=paranoid)
    from contextlib import ExitStack
    es = ExitStack()
    M = Mk(nc, es, nch, set(taps))
    M.serialize = serialize
    M.p.paranoid = paranoid
    with es:
        M.declare()
        M.body()
        M.p.emit_all(M.blk)
    return nc, M.tapnames


class Mk:
    def __init__(self, nc, es, nch, taps):
        self.nc = nc
        self.es = es
        self.nch = nch
        self.taps = taps
        self.tapnames = []
        self.p = Prog(nc)
        self.dep = {}

    def D(self, name):
        return self.dep.setdefault(name, Dep())

    serialize = False

    def op(self, eng, fn, reads=(), writes=(), wchk=()):
        p = self.p
        if self.serialize:
            for o in ENGS:
                p.need(eng, o, p.cnt[o])
            for st in p.DSTREAMS:
                p.need(eng, st, p.dmacnt[st])
        for r in reads:
            self.D(r).rd(p, eng)
        for w in wchk:
            self.D(w).wr(p, eng)
        for w in writes:
            self.D(w).wr(p, eng)
        tok = p.run(eng, fn)
        for r in reads:
            self.D(r).did_rd(tok)
        for w in writes:
            self.D(w).did_wr(tok)
        return tok

    def dma(self, dst, src, reads=(), writes=(), stream="small", queue="sp"):
        p = self.p
        if self.serialize:
            for o in ENGS:
                p.need(queue, o, p.cnt[o])
            for st in p.DSTREAMS:
                p.need(queue, st, p.dmacnt[st])
        for r in reads:
            self.D(r).rd(p, queue)
        for w in writes:
            self.D(w).wr(p, queue)
        tok = p.dma(dst, src, stream, queue)
        for r in reads:
            self.D(r).did_rd(tok)
        for w in writes:
            self.D(w).did_wr(tok)
        return tok

    def tap(self, name, ap, shape, deps):
        if name not in self.taps:
            return
        t = self.nc.dram_tensor("tap_" + name, shape, ap.dtype,
                                kind="ExternalOutput").ap()
        self.tapnames.append("tap_" + name)
        self.dma(t, ap, reads=deps, writes=(), stream="tap")

    def declare(self):
        nc, es = self.nc, self.es
        self.hbm = {}

        def hin(name, shape):
            self.hbm[name] = nc.dram_tensor(name, shape, F32,
                                            kind="ExternalInput").ap()
        hin("x_t", [D, Lseq])
        self.out_h = nc.dram_tensor("out_t", [D, Lseq], F32,
                                    kind="ExternalOutput").ap()
        self.big = {}
        for nm, (K, Mm) in {"inA": (D, 2 * E), "inB": (D, 2 * E),
                            "poA": (E, D), "poB": (E, D),
                            "W1": (D, H), "W2": (H, D)}.items():
            hin(nm + "_lhsT", [K, Mm])
            kb, mb = K // 128, Mm // 128
            self.big[nm] = dict(K=K, M=Mm, KB=kb, MB=mb,
                                scr=nc.dram_tensor("scr_" + nm,
                                                   [128, kb * Mm], BF16).ap())
        for nm, sh in [("xpA_lhsT", [E, 96]), ("xpB_lhsT", [E, 96]),
                       ("dtpA_lhsT", [RK, E]), ("dtpB_lhsT", [RK, E]),
                       ("Alog_r", [128, 2 * EB * NS]), ("dtpb_r", [128, 2 * EB]),
                       ("cw_r", [128, 2 * EB * 4]), ("cb_r", [128, 2 * EB]),
                       ("Dsk_r", [128, 2 * EB]),
                       ("lnw_r", [128, 2 * DB]), ("lnb_r", [128, 2 * DB]),
                       ("nfw_r", [128, DB]), ("nfb_r", [128, DB]),
                       ("b1_r", [128, HB]), ("b2_r", [128, DB]),
                       ("sc4_r", [4, 1]), ("sc16_r", [16, 1]),
                       ("onesmean_r", [128, 1]), ("onesrow_r", [1, 128]),
                       ("epsc_r", [128, 1]), ("nhalf_r", [128, 1]),
                       ("delta16_r", [16, NS * 128]), ("ident_r", [128, 128]),
                       ("mask1_r", [4, HB]), ("mask2_r", [4, DB])]:
            hin(nm, sh)

        s = {}

        def sb(name, shape, dt):
            s[name] = es.enter_context(nc.sbuf_tensor(name, shape, dt))
        sb("x_res", [128, DB * Lc], F32)
        sb("t_str", [128, DB * Lc], F32)
        sb("xnf", [128, DB * Lc], F32)
        sb("xin_bf", [128, DB * Lc], BF16)
        sb("y_bf", [128, EB * Lc], BF16)
        sb("xcin", [128, 2 * (Lc + 3)], F32)
        sb("tail3", [128, 2 * EB * 3], F32)
        sb("cconv", [128, 2 * Lc], F32)
        sb("sgtmp", [128, 2 * Lc], F32)
        sb("xc_bf", [128, EB * Lc], BF16)
        sb("zs_bf", [128, EB * Lc], BF16)
        sb("dtneg", [128, EB * Lc], F32)
        sb("dtraw_bf", [64, Lc], BF16)
        sb("Bneg_bf", [16, Lc], BF16)
        sb("C_bf", [16, Lc], BF16)
        sb("B_bc", [128, NS * Lc], BF16)
        sb("C_bc", [128, NS * Lc], BF16)
        sb("u_bf", [128, 2 * Lc], BF16)
        sb("dA", [128, NS * Lc], F32)
        sb("dBu", [128, NS * Lc], BF16)
        sb("h_bf", [128, NS * Lc], BF16)
        sb("ys_f", [128, 2 * Lc], F32)
        sb("ytmp", [128, 2 * Lc], F32)
        sb("carry", [128, 2 * EB * NS], F32)
        sb("hg_bf", [128, HB * Lc], BF16)
        sb("wbuf0", [128, MGRP * 8 * 128], BF16)
        sb("wbuf1", [128, MGRP * 8 * 128], BF16)
        sb("xp_sb", [128, 2 * EB * 96], BF16)
        sb("dtp_sb", [64, 2 * EB * 128], BF16)
        sb("Aneg", [128, 2 * EB * NS], F32)
        sb("ndtpb", [128, 2 * EB], F32)
        sb("cwsb", [128, 2 * EB * 4], F32)
        sb("cbc", [128, 2 * EB], F32)
        sb("Dsk", [128, 2 * EB], F32)
        sb("lnw", [128, 2 * DB], F32)
        sb("lnb", [128, 2 * DB], F32)
        sb("nfw", [128, DB], F32)
        sb("nfb", [128, DB], F32)
        sb("b1c", [128, HB], F32)
        sb("b2c", [128, DB], F32)
        sb("onesm", [128, 1], BF16)
        sb("epsc", [128, 1], F32)
        sb("nhalf", [128, 1], F32)
        sb("onesr_f", [1, 128], F32)
        sb("d16", [16, NS * 128], BF16)
        sb("sc4", [4, 1], F32)
        sb("sc16v", [16, 1], F32)
        sb("murow", [1, Lc], F32)
        sb("m2row", [1, Lc], F32)
        sb("vrow", [1, Lc], F32)
        sb("rrow", [1, Lc], F32)
        sb("ident", [128, 128], BF16)
        sb("mask1", [4, HB], F32)
        sb("mask2", [4, DB], F32)
        sb("u_lt", [128, DB * 16], BF16)
        sb("T1sb", [4, D], BF16)
        sb("T2sb", [4, D], BF16)
        sb("coef_bf", [4, Lc], BF16)
        sb("zbt", [4, 8 * 128], BF16)
        sb("g4", [4, 4 * Lc], BF16)
        self.s = s

        raw = {}

        def psd(name, shape):
            raw[name] = es.enter_context(nc.psum_tensor(name, shape, F32))
        psd("ps_a", [128, Lc])
        psd("ps_b", [128, Lc])
        psd("ps_x", [16, 2 * Lc])
        psd("ps_y", [128, 2 * Lc])
        psd("ps_o", [128, DB * Lc])
        ps = {
            "ps_a": raw["ps_a"][:, :],
            "ps_b": raw["ps_b"][:, :],
            "ps_bm": raw["ps_x"][0:16, 0:Lc],
            "ps_cm": raw["ps_x"][0:16, Lc:2 * Lc],
            "ps_s1": raw["ps_x"][0:1, 0:Lc],
            "ps_s2": raw["ps_x"][0:1, Lc:2 * Lc],
            "ps_bc": raw["ps_y"][:, 0:Lc],
            "ps_dt": raw["ps_y"][0:64, Lc:2 * Lc],
            "ps_o": raw["ps_o"][:, :],
        }
        self.ps = ps
        for alias, owner in (("ps_bm", "ps_x"), ("ps_cm", "ps_x"),
                             ("ps_s1", "ps_x"), ("ps_s2", "ps_x"),
                             ("ps_bc", "ps_y"), ("ps_dt", "ps_y")):
            self.dep[alias] = self.D(owner)

        self.p.alloc_sems(es)
        self.blk = es.enter_context(nc.Block())

    def bc8(self, psname):
        assert psname == "ps_bc"
        t = self.ps[psname].tensor
        return bass.AP(t, 0, [[2 * Lc, 128], [0, DB], [1, Lc]])

    def st0(self, ap_tensor, offset, reps, inner):
        return bass.AP(ap_tensor, offset, [[Lc, 128], [0, reps], [1, inner]])

    def body(self):
        self.p.scope = "prep"
        self.prepped = set()
        self.prep()
        for c in range(self.nch):
            self.chunk(c)
        self.p.scope = None
        p = self.p
        for st in p.DSTREAMS:
            v = p.dmacnt[st]
            if v:
                p.steps["sp"].append(
                    lambda h, sem=p.sems[st], v=v: h.wait_ge(sem, v))

    def prep(self):
        s, hbm, p = self.s, self.hbm, self.p
        op, dma = self.op, self.dma

        for src, dst in [("cw_r", "cwsb"), ("cb_r", "cbc"), ("Dsk_r", "Dsk"),
                         ("lnw_r", "lnw"), ("lnb_r", "lnb"), ("nfw_r", "nfw"),
                         ("nfb_r", "nfb"), ("b1_r", "b1c"), ("b2_r", "b2c"),
                         ("sc4_r", "sc4"), ("sc16_r", "sc16v"),
                         ("onesrow_r", "onesr_f"), ("mask1_r", "mask1"),
                         ("epsc_r", "epsc"), ("nhalf_r", "nhalf"),
                         ("mask2_r", "mask2")]:
            dma(s[dst][:, :], hbm[src], writes=(dst,))
        stg = s["dA"]
        dma(stg[:, 0:2 * EB * NS], hbm["Alog_r"], writes=("dA",))
        op("act", lambda h: h.activation(s["Aneg"][:, :], stg[:, 0:2 * EB * NS],
                                         AF.Exp),
           reads=("dA",), writes=("Aneg",))
        dma(stg[:, 0:2 * EB], hbm["dtpb_r"], writes=("dA",))
        op("dve", lambda h: h.tensor_scalar_mul(s["ndtpb"][:, :],
                                                stg[:, 0:2 * EB], -1.0),
           reads=("dA",), writes=("ndtpb",))
        dma(stg[:, 0:1], hbm["onesmean_r"], writes=("dA",))
        op("dve", lambda h: h.tensor_copy(s["onesm"][:, 0:1], stg[:, 0:1]),
           reads=("dA",), writes=("onesm",))
        dma(stg[0:16, 0:NS * 128], hbm["delta16_r"], writes=("dA",))
        op("dve", lambda h: h.tensor_copy(s["d16"][:, :], stg[0:16, 0:NS * 128]),
           reads=("dA",), writes=("d16",))
        dma(stg[:, 0:128], hbm["ident_r"], writes=("dA",))
        op("dve", lambda h: h.tensor_copy(s["ident"][:, :], stg[:, 0:128]),
           reads=("dA",), writes=("ident",))
        for L_, nm in ((0, "xpA_lhsT"), (1, "xpB_lhsT")):
            for k in range(EB):
                src2 = bass.AP(hbm[nm].tensor, k * 128 * 96, [[96, 128], [1, 96]])
                dma(stg[:, 0:96], src2, writes=("dA",))
                op("act", lambda h, L_=L_, k=k: h.activation(
                    s["xp_sb"][:, (L_ * EB + k) * 96:(L_ * EB + k + 1) * 96],
                    stg[:, 0:96], AF.Copy),
                   reads=("dA",), writes=("xp_sb",))
        for L_, nm in ((0, "dtpA_lhsT"), (1, "dtpB_lhsT")):
            dma(stg[0:64, 0:E], bass.AP(hbm[nm].tensor, 0, [[E, 64], [1, E]]),
                writes=("dA",))
            op("act", lambda h, L_=L_: h.activation(
                s["dtp_sb"][0:64, L_ * EB * 128:(L_ + 1) * EB * 128],
                stg[0:64, 0:E], AF.Copy),
               reads=("dA",), writes=("dtp_sb",))
        op("dve", lambda h: h.memset(s["tail3"][:, :], 0.0), writes=("tail3",))
        self.prep_slot = 0
        for nm in ("inA", "poA", "inB", "poB", "W1", "W2"):
            self.prep_big(nm)

    def prep_big(self, nm):
        if nm in self.prepped:
            return
        self.prepped.add(nm)
        s, hbm, p = self.s, self.hbm, self.p
        op, dma = self.op, self.dma
        stg = s["dA"]
        oldscope = p.scope
        p.scope = "prep"
        owners_a = ["dA"] + ["dA%d" % n for n in range(NS)]
        owners_b = ["dBu"]
        for sl in range(4):
            for nm2, owners in (("pstg%d" % sl, owners_a), ("pcv%d" % sl, owners_b)):
                d = self.D(nm2)
                for ow in owners:
                    o = self.D(ow)
                    for e, v in o.w.items():
                        d.w[e] = max(d.w.get(e, 0), v)
                    for e, v in o.r.items():
                        d.r[e] = max(d.r.get(e, 0), v)
        if True:
            info = self.big[nm]
            lhsT, scr = hbm[nm + "_lhsT"], info["scr"]
            KB, Mm = info["KB"], info["M"]
            kmajor = nm in ("poA", "poB")
            units = []
            if kmajor:
                for k in range(KB):
                    units.append((bass.AP(lhsT.tensor, k * 128 * Mm,
                                          [[Mm, 128], [1, Mm]]),
                                  k * Mm, Mm))
            else:
                mg = 32 // KB
                gw = mg * 128
                ngrp = info["MB"] // mg
                per = min(2048 // gw, ngrp)
                for gg in range(ngrp // per):
                    for k in range(KB):
                        units.append((bass.AP(lhsT.tensor,
                                              k * 128 * Mm + gg * per * gw,
                                              [[Mm, 128], [1, per * gw]]),
                                      None, (gg, k, per, gw)))
            if kmajor:
                gsz = 2 * Mm
            else:
                gsz = KB * (32 // KB) * 128
            ldstreams = ("wb0", "wb1", "small")
            ststreams = ("prep", "xin", "out")
            for src2, dsto, meta in units:
                sl = self.prep_slot % 2
                six = self.prep_slot % 3
                self.prep_slot += 1
                if dsto is not None:
                    w = meta
                    ldsl = stg[:, sl * 2048:sl * 2048 + w]
                    cvsl = s["dBu"][:, sl * 2048:sl * 2048 + w]
                    dma(ldsl, src2, writes=("pstg%d" % sl,),
                        stream=ldstreams[six], queue="sp")
                    op("dve", lambda h, ldsl=ldsl, cvsl=cvsl: h.tensor_copy(
                        cvsl, ldsl),
                       reads=("pstg%d" % sl,), writes=("pcv%d" % sl,))
                    dst = bass.AP(scr.tensor, dsto, [[KB * Mm, 128], [1, w]])
                    gdep = "scr_%s_%d" % (nm, dsto // gsz)
                    dma(dst, cvsl, reads=("pcv%d" % sl,), writes=(gdep,),
                        stream=ststreams[six], queue="act")
                else:
                    gg, k, per, gw = meta
                    w = per * gw
                    ldsl = stg[:, sl * 2048:sl * 2048 + w]
                    cvsl = s["dBu"][:, sl * 2048:sl * 2048 + w]
                    dma(ldsl, src2, writes=("pstg%d" % sl,),
                        stream=ldstreams[six], queue="sp")
                    op("dve", lambda h, ldsl=ldsl, cvsl=cvsl: h.tensor_copy(
                        cvsl, ldsl),
                       reads=("pstg%d" % sl,), writes=("pcv%d" % sl,))
                    grp0 = gg * per
                    dst = bass.AP(scr.tensor,
                                  grp0 * (KB * gw) + k * gw,
                                  [[KB * Mm, 128], [KB * gw, per], [1, gw]])
                    gdeps = tuple("scr_%s_%d" % (nm, grp0 + j) for j in range(per))
                    dma(dst, cvsl, reads=("pcv%d" % sl,), writes=gdeps,
                        stream=ststreams[six], queue="act")
        for sl in range(4):
            for nm2 in ("pstg%d" % sl, "pcv%d" % sl):
                d = self.D(nm2)
                for tok_eng, tok_v in list(d.w.items()) + list(d.r.items()):
                    for ow in ["dA", "dBu"] + ["dA%d" % n for n in range(NS)]:
                        self.D(ow).did_rd((tok_eng, tok_v))

    def ln_stats(self, src_name, bf_into="xin_bf"):
        s, ps, op = self.s, self.ps, self.op
        src = s[src_name]
        op("act", lambda h: h.activation(s[bf_into][:, :], src[:, :], AF.Copy),
           reads=(src_name,), writes=(bf_into,))
        op("act", lambda h: h.activation(s["hg_bf"][:, 0:DB * Lc], s[bf_into][:, :],
                                         AF.Square),
           reads=(bf_into,), writes=("hg_bf",))
        for k in range(DB):
            op("pe", lambda h, k=k: h.matmul(
                ps["ps_s1"][0:1, :], s["onesm"][:, 0:1],
                s[bf_into][:, k * Lc:(k + 1) * Lc],
                start=(k == 0), stop=(k == DB - 1)),
               reads=("onesm", bf_into),
               writes=("ps_s1",) if k == DB - 1 else (),
               wchk=("ps_s1",) if k == 0 else ())
        for k in range(DB):
            op("pe", lambda h, k=k: h.matmul(
                ps["ps_s2"][0:1, :], s["onesm"][:, 0:1],
                s["hg_bf"][:, k * Lc:(k + 1) * Lc],
                start=(k == 0), stop=(k == DB - 1)),
               reads=("onesm", "hg_bf"),
               writes=("ps_s2",) if k == DB - 1 else (),
               wchk=("ps_s2",) if k == 0 else ())
        op("dve", lambda h: h.tensor_copy(s["murow"][0:1, :], ps["ps_s1"][0:1, :]),
           reads=("ps_s1",), writes=("murow",))
        op("dve", lambda h: h.tensor_mul(s["vrow"][0:1, :], s["murow"][0:1, :],
                                         s["murow"][0:1, :]),
           reads=("murow",), writes=("vrow",))
        op("dve", lambda h: h.tensor_sub(s["vrow"][0:1, :], ps["ps_s2"][0:1, :],
                                         s["vrow"][0:1, :]),
           reads=("ps_s2", "vrow"), writes=("vrow",))
        op("act", lambda h: h.activation(s["vrow"][0:1, :], s["vrow"][0:1, :],
                                         AF.Ln, bias=s["epsc"][0:1, 0:1]),
           reads=("vrow", "epsc"), writes=("vrow",))
        op("act", lambda h: h.activation(s["rrow"][0:1, :], s["vrow"][0:1, :],
                                         AF.Exp, scale=s["nhalf"][0:1, 0:1]),
           reads=("vrow", "nhalf"), writes=("rrow",))

    def ln_bcast(self, rowname, psname="ps_bc"):
        s, ps, op = self.s, self.ps, self.op
        op("pe", lambda h: h.matmul(ps[psname][:, :], s["onesr_f"][0:1, :],
                                    s[rowname][0:1, :], start=True, stop=True),
           reads=("onesr_f", rowname), writes=(psname,), wchk=(psname,))

    def ln_apply(self, src_name, out_name, affine=None, out_dt_bf=True):
        s, op = self.s, self.op
        src, out = s[src_name], s[out_name]
        self.ln_bcast("murow")
        mu = self.bc8("ps_bc")
        op("dve", lambda h: h.tensor_sub(s["xnf"][:, :], src[:, :], mu),
           reads=(src_name, "ps_bc"), writes=("xnf",))
        self.ln_bcast("rrow")
        r = self.bc8("ps_bc")
        if affine is None:
            op("dve", lambda h: h.tensor_mul(out[:, :], s["xnf"][:, :], r),
               reads=("xnf", "ps_bc"), writes=(out_name,))
        else:
            wnm, bnm, woff = affine
            op("dve", lambda h: h.tensor_mul(s["xnf"][:, :], s["xnf"][:, :], r),
               reads=("xnf", "ps_bc"), writes=("xnf",))
            wap = self.st0w(s[wnm], woff)
            bap = self.st0w(s[bnm], woff)
            op("dve", lambda h: h.scalar_tensor_tensor(
                out[:, :], s["xnf"][:, :], 1.0, s["xnf"][:, :], ALU.mult,
                ALU.add) if False else h.tensor_scalar(
                out[:, :], s["xnf"][:, :], wap, bap, ALU.mult, ALU.add),
               reads=("xnf", wnm, bnm), writes=(out_name,))

    def st0w(self, tile, off):
        raise RuntimeError("unused")

    def chunk(self, c):
        s, p = self.s, self.p
        op, dma = self.op, self.dma
        src = bass.AP(self.hbm["x_t"].tensor, c * Lc,
                      [[Lseq, 128], [128 * Lseq, DB], [1, Lc]])
        dma(s["x_res"][:, :], src, writes=("x_res",), stream="xin")
        self.p.scope = "ln0"
        self.ln_stats("x_res")
        self.ln_apply("x_res", "t_str")
        for L_ in range(2):
            self.mamba_layer(c, L_)
        self.tap("c%d_t2" % c, s["t_str"][:, :], [128, DB * Lc], ("t_str",))
        self.p.scope = "normf"
        self.ln_stats("t_str")
        self.ln_bcast("murow")
        op("dve", lambda h: h.tensor_sub(s["xnf"][:, :], s["t_str"][:, :],
                                         self.bc8("ps_bc")),
           reads=("t_str", "ps_bc"), writes=("xnf",))
        self.ln_bcast("rrow")
        op("dve", lambda h: h.tensor_mul(s["xnf"][:, :], s["xnf"][:, :],
                                         self.bc8("ps_bc")),
           reads=("xnf", "ps_bc"), writes=("xnf",))
        for k in range(DB):
            op("dve", lambda h, k=k: h.tensor_scalar(
                s["xnf"][:, k * Lc:(k + 1) * Lc], s["xnf"][:, k * Lc:(k + 1) * Lc],
                s["nfw"][:, k:k + 1], s["nfb"][:, k:k + 1],
                ALU.mult, ALU.add),
               reads=("xnf", "nfw", "nfb"), writes=("xnf",))
        op("dve", lambda h: h.tensor_add(s["x_res"][:, :], s["x_res"][:, :],
                                         s["xnf"][:, :]),
           reads=("x_res", "xnf"), writes=("x_res",))
        self.ln_stats("x_res")
        self.ln_bcast("murow")
        op("dve", lambda h: h.tensor_sub(s["xnf"][:, :], s["x_res"][:, :],
                                         self.bc8("ps_bc")),
           reads=("x_res", "ps_bc"), writes=("xnf",))
        self.ln_bcast("rrow")
        op("dve", lambda h: h.tensor_mul(s["xin_bf"][:, :], s["xnf"][:, :],
                                         self.bc8("ps_bc")),
           reads=("xnf", "ps_bc"), writes=("xin_bf",))
        if c == 0:
            self.p.scope = "loraprep"
            self.lora_prep()
        self.p.scope = "ffn"
        self.ffn(c)
        dst = bass.AP(self.out_h.tensor, c * Lc,
                      [[Lseq, 128], [128 * Lseq, DB], [1, Lc]])
        dma(dst, s["x_res"][:, :], reads=("x_res",), stream="out")

    def stream_groups(self, nm):
        info = self.big[nm]
        KB = info["KB"]
        mg = 32 // KB
        for grp in range(info["MB"] // mg):
            bufn = "wbuf%d" % (grp % 2)
            buf = self.s[bufn]
            src = bass.AP(info["scr"].tensor, grp * (KB * mg * 128),
                          [[KB * info["M"], 128], [1, KB * mg * 128]])
            self.dma(buf[:, 0:KB * mg * 128], src,
                     reads=("scr_%s_%d" % (nm, grp),), writes=(bufn,),
                     stream="wb0" if grp % 2 == 0 else "wb1")

            def tile(k, mi, buf=buf, mg=mg):
                off = k * (mg * 128) + mi * 128
                return buf[:, off:off + 128]
            yield grp, mg, bufn, tile

    def stream_kgroups(self, nm, kg=2):
        info = self.big[nm]
        KB, MB = info["KB"], info["MB"]
        w = kg * MB * 128
        for grp in range(KB // kg):
            bufn = "wbuf%d" % (grp % 2)
            buf = self.s[bufn]
            src = bass.AP(info["scr"].tensor, grp * w,
                          [[KB * info["M"], 128], [1, w]])
            self.dma(buf[:, 0:w], src, reads=("scr_%s_%d" % (nm, grp),),
                     writes=(bufn,),
                     stream="wb0" if grp % 2 == 0 else "wb1")

            def tile(kig, m, buf=buf, MB=MB):
                off = kig * (MB * 128) + m * 128
                return buf[:, off:off + 128]
            yield grp, bufn, tile

    def mamba_layer(self, c, L_):
        s, ps, p = self.s, self.ps, self.p
        op = self.op
        nm_in = "inA" if L_ == 0 else "inB"
        nm_po = "poA" if L_ == 0 else "poB"
        lnoff = L_ * DB
        eoff = L_ * EB

        self.p.scope = "lnin"
        self.ln_stats("t_str")
        self.ln_bcast("murow")
        op("dve", lambda h: h.tensor_sub(s["xnf"][:, :], s["t_str"][:, :],
                                         self.bc8("ps_bc")),
           reads=("t_str", "ps_bc"), writes=("xnf",))
        self.ln_bcast("rrow")
        op("dve", lambda h: h.tensor_mul(s["xnf"][:, :], s["xnf"][:, :],
                                         self.bc8("ps_bc")),
           reads=("xnf", "ps_bc"), writes=("xnf",))
        for k in range(DB):
            op("dve", lambda h, k=k: h.tensor_scalar(
                s["xin_bf"][:, k * Lc:(k + 1) * Lc],
                s["xnf"][:, k * Lc:(k + 1) * Lc],
                s["lnw"][:, lnoff + k:lnoff + k + 1],
                s["lnb"][:, lnoff + k:lnoff + k + 1],
                ALU.mult, ALU.add),
               reads=("xnf", "lnw", "lnb"), writes=("xin_bf",))

        self.tap("c%dL%d_tailpre" % (c, L_), s["tail3"][:, :],
                 [128, 2 * EB * 3], ("tail3",))
        self.p.scope = "inproj"
        for grp, mg, bufn, tile in self.stream_groups(nm_in):
            for mi in range(mg):
                m = grp * mg + mi
                bank = "ps_a" if m % 2 == 0 else "ps_b"
                for k in range(DB):
                    op("pe", lambda h, k=k, mi=mi, bank=bank, tile=tile: h.matmul(
                        ps[bank][:, :], tile(k, mi),
                        s["xin_bf"][:, k * Lc:(k + 1) * Lc],
                        start=(k == 0), stop=(k == DB - 1)),
                       reads=(bufn, "xin_bf"),
                       writes=(bank,) if k == DB - 1 else (),
                       wchk=(bank,) if k == 0 else ())
                self.inproj_evac(c, L_, m, bank, eoff)

        self.tap("c%dL%d_xin" % (c, L_), s["xin_bf"][:, :], [128, DB * Lc],
                 ("xin_bf",))
        self.p.scope = "xproj"
        for k in range(EB):
            xoff = (L_ * EB + k) * 96
            for tgt, dep, c0, c1 in ((ps["ps_a"][0:64, :], "ps_a", 0, 64),
                                     (ps["ps_bm"][0:16, :], "ps_x", 64, 80),
                                     (ps["ps_b"][0:16, :], "ps_b", 80, 96)):
                op("pe", lambda h, k=k, xoff=xoff, tgt=tgt, c0=c0, c1=c1: h.matmul(
                    tgt, s["xp_sb"][:, xoff + c0:xoff + c1],
                    s["xc_bf"][:, k * Lc:(k + 1) * Lc],
                    start=(k == 0), stop=(k == EB - 1)),
                   reads=("xp_sb", "xc_bf"),
                   writes=(dep,) if k == EB - 1 else (),
                   wchk=(dep,) if k == 0 else ())
        op("act", lambda h: h.activation(s["dtraw_bf"][0:64, :], ps["ps_a"][0:64, :],
                                         AF.Copy),
           reads=("ps_a",), writes=("dtraw_bf",))
        op("dve", lambda h: h.tensor_scalar_mul(s["Bneg_bf"][0:16, :],
                                                ps["ps_bm"][0:16, :], -1.0),
           reads=("ps_x",), writes=("Bneg_bf",))
        op("act", lambda h: h.activation(s["C_bf"][0:16, :], ps["ps_b"][0:16, :],
                                         AF.Copy),
           reads=("ps_b",), writes=("C_bf",))
        self.p.scope = "bcbuild"
        for n in range(NS):
            op("pe", lambda h, n=n: h.matmul(
                ps["ps_bc"][:, :], s["d16"][:, n * 128:(n + 1) * 128],
                s["Bneg_bf"][0:16, :], start=True, stop=True),
               reads=("d16", "Bneg_bf"), writes=("ps_bc",), wchk=("ps_bc",))
            op("act", lambda h, n=n: h.activation(
                s["B_bc"][:, n * Lc:(n + 1) * Lc], ps["ps_bc"][:, :], AF.Copy),
               reads=("ps_bc",), writes=("B_bc",))
            op("pe", lambda h, n=n: h.matmul(
                ps["ps_bc"][:, :], s["d16"][:, n * 128:(n + 1) * 128],
                s["C_bf"][0:16, :], start=True, stop=True),
               reads=("d16", "C_bf"), writes=("ps_bc",), wchk=("ps_bc",))
            op("act", lambda h, n=n: h.activation(
                s["C_bc"][:, n * Lc:(n + 1) * Lc], ps["ps_bc"][:, :], AF.Copy),
               reads=("ps_bc",), writes=("C_bc",))

        self.tap("c%dL%d_xc" % (c, L_), s["xc_bf"][:, :], [128, EB * Lc],
                 ("xc_bf",))
        self.tap("c%dL%d_zs" % (c, L_), s["zs_bf"][:, :], [128, EB * Lc],
                 ("zs_bf",))
        self.tap("c%dL%d_Bbc" % (c, L_), s["B_bc"][:, :], [128, NS * Lc],
                 ("B_bc",))
        self.tap("c%dL%d_Cbc" % (c, L_), s["C_bc"][:, :], [128, NS * Lc],
                 ("C_bc",))
        self.p.scope = "dtproj"
        for m in range(EB):
            bank = "ps_a" if m % 2 == 0 else "ps_b"
            op("pe", lambda h, m=m, bank=bank: h.matmul(
                ps[bank][:, :], s["dtp_sb"][0:64, (L_ * EB + m) * 128:
                                            (L_ * EB + m + 1) * 128],
                s["dtraw_bf"][0:64, :], start=True, stop=True),
               reads=("dtp_sb", "dtraw_bf"), writes=(bank,), wchk=(bank,))
            op("act", lambda h, m=m, bank=bank: h.activation(
                s["dtneg"][:, m * Lc:(m + 1) * Lc], ps[bank][:, :], AF.Sigmoid,
                scale=-1.0, bias=s["ndtpb"][:, eoff + m:eoff + m + 1]),
               reads=(bank, "ndtpb"), writes=("dtneg",))
        op("act", lambda h: h.activation(s["dtneg"][:, :], s["dtneg"][:, :], AF.Ln),
           reads=("dtneg",), writes=("dtneg",))

        self.tap("c%dL%d_dtneg" % (c, L_), s["dtneg"][:, :], [128, EB * Lc],
                 ("dtneg",))
        self.p.scope = "scan"
        po_iter = self.stream_kgroups(nm_po, kg=2)
        po_cur = None
        for blk in range(EB):
            self.scan_block(c, L_, blk, eoff)
            kig = blk % 2
            if kig == 0:
                po_cur = next(po_iter)
            grp, bufn, tile = po_cur
            for m in range(DB):
                op("pe", lambda h, kig=kig, m=m, tile=tile, blk=blk: h.matmul(
                    ps["ps_o"][:, m * Lc:(m + 1) * Lc], tile(kig, m),
                    s["y_bf"][:, blk * Lc:(blk + 1) * Lc],
                    start=(blk == 0 and m % 2 == 0),
                    stop=(blk == EB - 1 and m % 2 == 1)),
                   reads=(bufn, "y%d" % blk),
                   writes=("ps_o",) if (blk == EB - 1 and m == DB - 1) else (),
                   wchk=("ps_o",) if (blk == 0 and m == 0) else ())
        self.tap("c%dL%d_y" % (c, L_), s["y_bf"][:, :], [128, EB * Lc],
                 tuple("y%d" % b for b in range(EB)))
        self.tap("c%dL%d_carry" % (c, L_), s["carry"][:, :], [128, 2 * EB * NS],
                 ("carry",))

        self.p.scope = "outproj"
        for m in range(DB):
            op("dve", lambda h, m=m: h.tensor_add(
                s["t_str"][:, m * Lc:(m + 1) * Lc],
                s["t_str"][:, m * Lc:(m + 1) * Lc],
                ps["ps_o"][:, m * Lc:(m + 1) * Lc]),
               reads=("t_str", "ps_o"), writes=("t_str",))

    def inproj_evac(self, c, L_, m, bank, eoff):
        s, ps, op = self.s, self.ps, self.op
        if m < EB:
            blk = m
            rot = blk % 2
            xoff = rot * (Lc + 3)
            op("dve", lambda h, blk=blk, xoff=xoff: h.tensor_copy(
                s["xcin"][:, xoff:xoff + 3],
                s["tail3"][:, (L_ * EB + blk) * 3:(L_ * EB + blk + 1) * 3]),
               reads=("tail3",), writes=("xcin%d" % rot,))
            op("dve", lambda h, xoff=xoff, bank=bank: h.tensor_copy(
                s["xcin"][:, xoff + 3:xoff + 3 + Lc], ps[bank][:, :]),
               reads=(bank,), writes=("xcin%d" % rot,))
            op("dve", lambda h, blk=blk, bank=bank: h.tensor_copy(
                s["tail3"][:, (L_ * EB + blk) * 3:(L_ * EB + blk + 1) * 3],
                ps[bank][:, Lc - 3:Lc]),
               reads=(bank,), writes=("tail3",))
            cvo = (L_ * EB + blk) * 4
            op("dve", lambda h, xoff=xoff, rot=rot, cvo=cvo, blk=blk: h.tensor_scalar(
                s["cconv"][:, rot * Lc:(rot + 1) * Lc],
                s["xcin"][:, xoff:xoff + Lc],
                s["cwsb"][:, cvo:cvo + 1], s["cbc"][:, eoff + blk:eoff + blk + 1],
                ALU.mult, ALU.add),
               reads=("xcin%d" % rot, "cwsb", "cbc"), writes=("cconv%d" % rot,))
            for kk in range(1, 4):
                op("dve", lambda h, xoff=xoff, rot=rot, cvo=cvo, kk=kk:
                   h.scalar_tensor_tensor(
                       s["cconv"][:, rot * Lc:(rot + 1) * Lc],
                       s["xcin"][:, xoff + kk:xoff + kk + Lc],
                       s["cwsb"][:, cvo + kk:cvo + kk + 1],
                       s["cconv"][:, rot * Lc:(rot + 1) * Lc],
                       ALU.mult, ALU.add),
                   reads=("xcin%d" % rot, "cwsb", "cconv%d" % rot),
                   writes=("cconv%d" % rot,))
            op("act", lambda h, rot=rot: h.activation(
                s["sgtmp"][:, rot * Lc:(rot + 1) * Lc],
                s["cconv"][:, rot * Lc:(rot + 1) * Lc], AF.Sigmoid),
               reads=("cconv%d" % rot,), writes=("sgtmp%d" % rot,))
            op("dve", lambda h, rot=rot, blk=blk: h.tensor_mul(
                s["xc_bf"][:, blk * Lc:(blk + 1) * Lc],
                s["cconv"][:, rot * Lc:(rot + 1) * Lc],
                s["sgtmp"][:, rot * Lc:(rot + 1) * Lc]),
               reads=("cconv%d" % rot, "sgtmp%d" % rot), writes=("xc_bf",))
        else:
            blk = m - EB
            rot = blk % 2
            op("act", lambda h, rot=rot, bank=bank: h.activation(
                s["sgtmp"][:, rot * Lc:(rot + 1) * Lc], ps[bank][:, :], AF.Sigmoid),
               reads=(bank,), writes=("sgtmp%d" % rot,))
            op("dve", lambda h, rot=rot, bank=bank, blk=blk: h.tensor_mul(
                s["zs_bf"][:, blk * Lc:(blk + 1) * Lc], ps[bank][:, :],
                s["sgtmp"][:, rot * Lc:(rot + 1) * Lc]),
               reads=(bank, "sgtmp%d" % rot), writes=("zs_bf",))

    def scan_block(self, c, L_, blk, eoff):
        s, op = self.s, self.op
        rot = blk % 2
        aoff = (L_ * EB + blk) * NS
        if rot == 0:
            dA = s["dA"][:, :]
            dAdep = "dA"
        else:
            dA = s["hg_bf"][:, :].bitcast(F32)[:, 0:NS * Lc]
            dAdep = "hgA"
            base = self.D("hg_bf")
            for n in range(NS):
                d = Dep()
                d.w = dict(base.w)
                d.r = dict(base.r)
                self.dep["hgA%d" % n] = d
        for n in range(NS):
            op("act", lambda h, n=n, blk=blk, aoff=aoff, dA=dA: h.activation(
                dA[:, n * Lc:(n + 1) * Lc],
                s["dtneg"][:, blk * Lc:(blk + 1) * Lc], AF.Exp,
                scale=s["Aneg"][:, aoff + n:aoff + n + 1]),
               reads=("dtneg", "Aneg"), writes=("%s%d" % (dAdep, n),))
        op("dve", lambda h, blk=blk, rot=rot: h.tensor_mul(
            s["u_bf"][:, rot * Lc:(rot + 1) * Lc],
            s["dtneg"][:, blk * Lc:(blk + 1) * Lc],
            s["xc_bf"][:, blk * Lc:(blk + 1) * Lc]),
           reads=("dtneg", "xc_bf"), writes=("u%d" % rot,))
        u0 = bass.AP(s["u_bf"][:, :].tensor, rot * Lc,
                     [[2 * Lc, 128], [0, NS], [1, Lc]])
        import os as _os2
        dbu_eng = "pool" if _os2.environ.get("DBU_POOL", "0") == "1" else "dve"
        op(dbu_eng, lambda h, u0=u0: h.tensor_mul(s["dBu"][:, :], u0, s["B_bc"][:, :]),
           reads=("u%d" % rot, "B_bc"), writes=("dBu",))
        for n in range(NS):
            if c == 0:
                init = 0.0
            else:
                init = s["carry"][:, (aoff + n):(aoff + n) + 1]
            op("dve", lambda h, n=n, init=init, dA=dA: h.tensor_tensor_scan(
                s["h_bf"][:, n * Lc:(n + 1) * Lc],
                dA[:, n * Lc:(n + 1) * Lc],
                s["dBu"][:, n * Lc:(n + 1) * Lc],
                init, ALU.mult, ALU.add),
               reads=("%s%d" % (dAdep, n), "dBu") + (("carry",) if c > 0 else ()),
               writes=("h_bf",))
        op("dve", lambda h: h.tensor_mul(s["dBu"][:, :], s["h_bf"][:, :],
                                         s["C_bc"][:, :]),
           reads=("h_bf", "C_bc"), writes=("dBu",))
        hv = bass.AP(s["dBu"][:, :].tensor, 0, [[NS * Lc, 128], [1, Lc], [Lc, NS]])
        op("dve", lambda h, hv=hv, rot=rot: h.tensor_reduce(
            s["ys_f"][:, rot * Lc:(rot + 1) * Lc], hv, AX.X, ALU.add),
           reads=("dBu",), writes=("ys%d" % rot,))
        hlast = bass.AP(s["h_bf"][:, :].tensor, Lc - 1, [[NS * Lc, 128], [Lc, NS]])
        op("dve", lambda h, hlast=hlast, aoff=aoff: h.tensor_copy(
            s["carry"][:, aoff:aoff + NS], hlast),
           reads=("h_bf",), writes=("carry",))
        op("dve", lambda h, rot=rot, blk=blk: h.scalar_tensor_tensor(
            s["ytmp"][:, rot * Lc:(rot + 1) * Lc],
            s["xc_bf"][:, blk * Lc:(blk + 1) * Lc],
            s["Dsk"][:, eoff + blk:eoff + blk + 1],
            s["ys_f"][:, rot * Lc:(rot + 1) * Lc], ALU.mult, ALU.add),
           reads=("xc_bf", "Dsk", "ys%d" % rot), writes=("yt%d" % rot,))
        op("dve", lambda h, rot=rot, blk=blk: h.tensor_mul(
            s["y_bf"][:, blk * Lc:(blk + 1) * Lc],
            s["ytmp"][:, rot * Lc:(rot + 1) * Lc],
            s["zs_bf"][:, blk * Lc:(blk + 1) * Lc]),
           reads=("yt%d" % rot, "zs_bf"), writes=("y%d" % blk,))
        if rot == 1:
            base = self.D("hg_bf")
            for n in range(NS):
                d = self.dep["hgA%d" % n]
                for e, v in d.w.items():
                    base.w[e] = max(base.w.get(e, 0), v)
                for e, v in d.r.items():
                    base.r[e] = max(base.r.get(e, 0), v)

    def lora_prep(self):
        s, ps, op = self.s, self.ps, self.op
        src = bass.AP(s["xin_bf"][:, :].tensor, 1, [[DB * Lc, 128], [Lc, DB], [1, 16]])
        op("dve", lambda h, src=src: h.tensor_copy(s["u_lt"][:, :], src),
           reads=("xin_bf",), writes=("u_lt",))
        for tgt, base in (("T1sb", 0), ("T2sb", 8)):
            for blk in range(DB):
                inap = bass.AP(s["u_lt"][:, :].tensor, blk * 16 + base,
                               [[DB * 16, 128], [2, 4]])
                trout = ps["ps_bm"][0:4, 0:64].bitcast(BF16)
                op("pe", lambda h, inap=inap, trout=trout: h.transpose(
                    trout, inap, s["ident"][:, :]),
                   reads=("u_lt", "ident"), writes=("ps_bm",), wchk=("ps_bm",))
                op("act", lambda h, tgt=tgt, blk=blk, trout=trout: h.activation(
                    s[tgt][0:4, blk * 128:(blk + 1) * 128], trout, AF.Copy),
                   reads=("ps_bm",), writes=(tgt,))


    def ffn(self, c):
        s, ps, op = self.s, self.ps, self.op
        for k in range(DB):
            st = bass.AP(s["u_lt"][:, :].tensor, k * 16 + 1, [[DB * 16, 128], [2, 4]])
            op("pe", lambda h, k=k, st=st: h.matmul(
                ps["ps_bm"][0:4, :], st, s["xin_bf"][:, k * Lc:(k + 1) * Lc],
                start=(k == 0), stop=(k == DB - 1)),
               reads=("u_lt", "xin_bf"), writes=("ps_bm",) if k == DB - 1 else (),
               wchk=("ps_bm",) if k == 0 else ())
        op("dve", lambda h: h.tensor_scalar_mul(s["coef_bf"][0:4, :],
                                                ps["ps_bm"][0:4, :],
                                                s["sc4"][0:4, 0:1]),
           reads=("ps_bm", "sc4"), writes=("coef_bf",))
        for grp, mg, bufn, tile in self.stream_groups("W1"):
            for mi in range(mg):
                m = grp * mg + mi
                bank = "ps_a" if m % 2 == 0 else "ps_b"
                for k in range(DB):
                    op("pe", lambda h, k=k, mi=mi, bank=bank, tile=tile: h.matmul(
                        ps[bank][:, :], tile(k, mi),
                        s["xin_bf"][:, k * Lc:(k + 1) * Lc],
                        start=(k == 0), stop=False),
                       reads=(bufn, "xin_bf"),
                       wchk=(bank,) if k == 0 else ())
                zr = (m % 2) * 512
                op("dve", lambda h, m=m, zr=zr: h.tensor_scalar_mul(
                    s["zbt"][0:4, zr:zr + 128],
                    s["T1sb"][0:4, (m % 8) * 128:(m % 8 + 1) * 128],
                    s["mask1"][0:4, m:m + 1]),
                   reads=("T1sb", "mask1"), writes=("zbt%d" % (m % 2),))
                op("pe", lambda h, m=m, bank=bank, zr=zr: h.matmul(
                    ps[bank][:, :], s["zbt"][0:4, zr:zr + 128],
                    s["coef_bf"][0:4, :], start=False, stop=True),
                   reads=("zbt%d" % (m % 2), "coef_bf"), writes=(bank,))
                op("act", lambda h, m=m, bank=bank: h.activation(
                    s["hg_bf"][:, m * Lc:(m + 1) * Lc], ps[bank][:, :], AF.Gelu,
                    bias=s["b1c"][:, m:m + 1]),
                   reads=(bank, "b1c"), writes=("hg_bf",))
        gbanks = ("ps_bm", "ps_cm", "ps_bc", "ps_dt")
        for sp_ in range(4):
            psn = gbanks[sp_]
            for i, kb in enumerate(range(sp_ * 8, sp_ * 8 + 8)):
                st = bass.AP(s["u_lt"][:, :].tensor, (kb % 8) * 16 + 9,
                             [[DB * 16, 128], [2, 4]])
                op("pe", lambda h, kb=kb, st=st, psn=psn, i=i: h.matmul(
                    self.ps[psn][0:4, :],
                    st, s["hg_bf"][:, kb * Lc:(kb + 1) * Lc],
                    start=(i == 0), stop=(i == 7)),
                   reads=("u_lt", "hg_bf"), writes=(psn,) if i == 7 else (),
                   wchk=(psn,) if i == 0 else ())
            op("dve", lambda h, sp_=sp_, psn=psn: h.tensor_scalar_mul(
                s["g4"][0:4, sp_ * Lc:(sp_ + 1) * Lc],
                self.ps[psn][0:4, :],
                s["sc4"][0:4, 0:1]),
               reads=(psn, "sc4"), writes=("g4",))
        for grp, mg, bufn, tile in self.stream_groups("W2"):
            for mi in range(mg):
                m = grp * mg + mi
                bank = "ps_a" if m % 2 == 0 else "ps_b"
                for k in range(HB):
                    op("pe", lambda h, k=k, mi=mi, bank=bank, tile=tile: h.matmul(
                        ps[bank][:, :], tile(k, mi),
                        s["hg_bf"][:, k * Lc:(k + 1) * Lc],
                        start=(k == 0), stop=False),
                       reads=(bufn, "hg_bf"),
                       wchk=(bank,) if k == 0 else ())
                zr = (m % 2) * 512
                for sp_ in range(4):
                    gat = bass.AP(s["T2sb"][:, :].tensor, (m % 2) * 512 + sp_,
                                  [[D, 4], [4, 128]])
                    op("dve", lambda h, m=m, sp_=sp_, gat=gat, zr=zr:
                       h.tensor_scalar_mul(
                           s["zbt"][0:4, zr + sp_ * 128:zr + (sp_ + 1) * 128],
                           gat, s["mask2"][0:4, m:m + 1]),
                       reads=("T2sb", "mask2"), writes=("zbt%d" % (m % 2),))
                for sp_ in range(4):
                    op("pe", lambda h, m=m, sp_=sp_, bank=bank, zr=zr: h.matmul(
                        ps[bank][:, :],
                        s["zbt"][0:4, zr + sp_ * 128:zr + (sp_ + 1) * 128],
                        s["g4"][0:4, sp_ * Lc:(sp_ + 1) * Lc],
                        start=False, stop=(sp_ == 3)),
                       reads=("zbt%d" % (m % 2), "g4"),
                       writes=(bank,) if sp_ == 3 else ())
                op("dve", lambda h, m=m, bank=bank: h.scalar_tensor_tensor(
                    s["x_res"][:, m * Lc:(m + 1) * Lc], ps[bank][:, :],
                    s["b2c"][:, m:m + 1],
                    s["x_res"][:, m * Lc:(m + 1) * Lc], ALU.add, ALU.add),
                   reads=(bank, "b2c", "x_res"), writes=("x_res",))


def make_inputs(inp):
    f = np.float32
    A = np.ascontiguousarray
    outs = []
    const = {
        "sc4_r": np.repeat(inp["L0_ffn_scale"].reshape(1, 1), 4, 0).astype(f),
        "sc16_r": np.repeat(inp["L0_ffn_scale"].reshape(1, 1), 16, 0).astype(f),
        "onesmean_r": np.full((128, 1), 1.0 / D, f),
        "epsc_r": np.full((128, 1), EPS, f),
        "nhalf_r": np.full((128, 1), -0.5, f),
        "onesrow_r": np.ones((1, 128), f),
        "ident_r": np.eye(128, dtype=f),
    }
    d16 = np.zeros((16, NS * 128), f)
    for n in range(NS):
        d16[n, n * 128:(n + 1) * 128] = 1.0
    const["delta16_r"] = d16
    m1 = np.zeros((4, HB), f)
    for b2 in range(HB):
        m1[b2 // 8, b2] = 1.0
    const["mask1_r"] = m1
    m2 = np.zeros((4, DB), f)
    for b in range(DB):
        m2[b // 2, b] = 1.0
    const["mask2_r"] = m2

    def r3(x, nb):
        return A(x.reshape(2, nb, 128).transpose(2, 0, 1).reshape(128, 2 * nb))

    def r1(x, nb):
        return A(x.reshape(nb, 128).T)

    w = {
        "inA_lhsT": A(inp["L0_attn_in_proj_w"][0].T),
        "inB_lhsT": A(inp["L0_attn_in_proj_w"][1].T),
        "poA_lhsT": A(inp["L0_attn_out_proj_w"][0].T),
        "poB_lhsT": A(inp["L0_attn_out_proj_w"][1].T),
        "W1_lhsT": A(inp["L0_ffn_W1"].T),
        "W2_lhsT": A(inp["L0_ffn_W2"].T),
        "xpA_lhsT": A(inp["L0_attn_x_proj_w"][0].T),
        "xpB_lhsT": A(inp["L0_attn_x_proj_w"][1].T),
        "dtpA_lhsT": A(inp["L0_attn_dt_proj_w"][0].T),
        "dtpB_lhsT": A(inp["L0_attn_dt_proj_w"][1].T),
        "Alog_r": A(inp["L0_attn_A_log"].reshape(2, EB, 128, NS)
                    .transpose(2, 0, 1, 3).reshape(128, 2 * EB * NS)),
        "dtpb_r": r3(inp["L0_attn_dt_proj_b"], EB),
        "cw_r": A(inp["L0_attn_conv_w"].reshape(2, EB, 128, 4)
                  .transpose(2, 0, 1, 3).reshape(128, 2 * EB * 4)),
        "cb_r": r3(inp["L0_attn_conv_b"], EB),
        "Dsk_r": r3(inp["L0_attn_D_skip"], EB),
        "lnw_r": r3(inp["L0_attn_ln_w"], DB),
        "lnb_r": r3(inp["L0_attn_ln_b"], DB),
        "nfw_r": r1(inp["L0_attn_norm_f_w"], DB),
        "nfb_r": r1(inp["L0_attn_norm_f_b"], DB),
        "b1_r": r1(inp["L0_ffn_b1"], HB),
        "b2_r": r1(inp["L0_ffn_b2"], DB),
    }
    w.update(const)
    x = inp["L0_attn_x"]
    in_maps = []
    for b in range(x.shape[0]):
        m = dict(w)
        m["x_t"] = A(x[b].T)
        in_maps.append(m)
    return in_maps


def kernel(**inputs):
    in_maps = make_inputs(inputs)
    nc, _ = build()
    res = run_bass_kernel_spmd(nc, in_maps, core_ids=list(range(len(in_maps))))
    return np.stack([np.ascontiguousarray(r["out_t"].T) for r in res.results])


# version 11
# speedup vs baseline: 1.1995x; 1.1111x over previous
import numpy as np
import concourse.bass as bass
import concourse.mybir as mybir
from concourse.bass_utils import run_bass_kernel_spmd
from concourse import dve_ops as _dve_ops
from concourse import dve_spec as _dve_spec
from concourse.dve_op import DveOp as _DveOp


def _mk_mulscan():
    name = "ANT_K_MULSCAN"
    for o in _dve_ops.OPS:
        if o.name == name:
            return o
    spec = _dve_spec.Spec(body=_dve_spec.scan(
        _dve_spec.AluOp.ADD, _dve_spec.Src0 * _dve_spec.Src1))
    shas = {}
    for ver in ("v3", "v4"):
        try:
            shas[ver] = _dve_spec.lower(spec, ver=ver, perf_en=False,
                                        name=name).sha(ver)
        except Exception:
            pass

    def _ref(in0, in1, c0, c1, c2):
        dt_ = np.float64 if in0.dtype == np.float64 else np.float32
        return np.cumsum(in0.astype(dt_) * in1.astype(dt_), axis=1)

    op = _DveOp(name, spec, subdim=False, uops_sha=shas, reference=_ref)
    _dve_ops.OPS.append(op)
    _dve_ops.CUSTOM_DVE_SPECS[name] = spec
    _dve_ops.CUSTOM_DVE_REFS[name] = _ref
    _dve_ops._SUB_OPCODE_FOR_NAME[name] = (
        _dve_ops._CUSTOM_DVE_ROW_BASE + len(_dve_ops.OPS) - 1)
    return op


MULSCAN = _mk_mulscan()

F32 = mybir.dt.float32
BF16 = mybir.dt.bfloat16
AF = mybir.ActivationFunctionType
ALU = mybir.AluOpType
AX = mybir.AxisListType

D, E, NS, RK, Lseq, H = 1024, 2048, 16, 64, 4096, 4096
DB, EB, HB = D // 128, E // 128, H // 128
EPS = 1e-5
Lc = 256
NCH = Lseq // Lc
MGRP = 4

ENGS = ("pe", "act", "dve", "pool", "sp")


class Prog:

    DSTREAMS = ("dma:wb0", "dma:wb1", "dma:xin", "dma:out", "dma:prep",
                "dma:small", "dma:tap")

    def __init__(self, nc):
        self.nc = nc
        self.scope = None
        self.steps = {e: [] for e in ENGS}
        self.cnt = {e: 0 for e in ENGS}
        self.dmacnt = {st: 0 for st in self.DSTREAMS}
        self.water = {e: {p: 0 for p in ENGS + self.DSTREAMS} for e in ENGS}
        self.sems = {}

    def alloc_sems(self, es):
        for e in ENGS:
            self.sems[e] = es.enter_context(self.nc.semaphore(f"S{e}"))
        for i, st in enumerate(self.DSTREAMS):
            self.sems[st] = es.enter_context(self.nc.semaphore(f"Sd{i}"))

    paranoid = False

    def need(self, eng, prod, val):
        if val <= 0 or self.water[eng][prod] >= val:
            return
        if prod == eng and not self.paranoid:
            return
        self.water[eng][prod] = val
        sem = self.sems[prod]
        self.steps[eng].append(lambda h, sem=sem, val=val: h.wait_ge(sem, val))

    def run(self, eng, fn):
        self.cnt[eng] += 1
        sem = self.sems[eng]

        def emit(h, fn=fn, sem=sem):
            fn(h).then_inc(sem, 1)
        emit._scope = self.scope
        self.steps[eng].append(emit)
        return (eng, self.cnt[eng])

    def dma(self, dst, src, stream="small", queue="sp"):
        st = "dma:" + stream
        sem = self.sems[st]
        prev = self.dmacnt[st]
        if prev:
            self.need(queue, st, prev)
        self.dmacnt[st] = prev + 16

        def emit(h, dst=dst, src=src, sem=sem):
            h.dma_start(dst, src).then_inc(sem, 16)
        emit._scope = self.scope
        self.steps[queue].append(emit)
        return (st, self.dmacnt[st])

    def emit_all(self, block):
        nc = self.nc
        for eng, attr in (("pe", "tensor"), ("act", "scalar"), ("dve", "vector"),
                          ("pool", "gpsimd"), ("sp", "sync")):
            steps = self.steps[eng]

            def body(h, steps=steps):
                cur = None
                cur_id = None
                for st in steps:
                    sc = getattr(st, "_scope", None)
                    if sc != cur:
                        if cur is not None:
                            nc.leave_named_scope(cur, cur_id, False)
                        if sc is not None:
                            cur_id, _ = nc.enter_named_scope(sc, False)
                        cur = sc
                    st(h)
                if cur is not None:
                    nc.leave_named_scope(cur, cur_id, False)
            getattr(block, attr)(body)


class Dep:

    def __init__(self):
        self.w = {}
        self.r = {}

    def rd(self, p, eng):
        for pe, v in self.w.items():
            p.need(eng, pe, v)

    def wr(self, p, eng):
        for pe, v in self.w.items():
            p.need(eng, pe, v)
        for pe, v in self.r.items():
            p.need(eng, pe, v)

    def did_rd(self, tok):
        eng, v = tok
        self.r[eng] = max(self.r.get(eng, 0), v)

    def did_wr(self, tok):
        eng, v = tok
        self.w[eng] = max(self.w.get(eng, 0), v)


def build(nch=NCH, taps=(), serialize=False, paranoid=False):
    nc = bass.Bass("TRN2", target_bir_lowering=False,
                   detect_race_conditions=paranoid)
    from contextlib import ExitStack
    es = ExitStack()
    M = Mk(nc, es, nch, set(taps))
    M.serialize = serialize
    M.p.paranoid = paranoid
    with es:
        M.declare()
        M.body()
        M.p.emit_all(M.blk)
    return nc, M.tapnames


class Mk:
    def __init__(self, nc, es, nch, taps):
        self.nc = nc
        self.es = es
        self.nch = nch
        self.taps = taps
        self.tapnames = []
        self.p = Prog(nc)
        self.dep = {}

    def D(self, name):
        return self.dep.setdefault(name, Dep())

    serialize = False

    def op(self, eng, fn, reads=(), writes=(), wchk=()):
        p = self.p
        if self.serialize:
            for o in ENGS:
                p.need(eng, o, p.cnt[o])
            for st in p.DSTREAMS:
                p.need(eng, st, p.dmacnt[st])
        for r in reads:
            self.D(r).rd(p, eng)
        for w in wchk:
            self.D(w).wr(p, eng)
        for w in writes:
            self.D(w).wr(p, eng)
        tok = p.run(eng, fn)
        for r in reads:
            self.D(r).did_rd(tok)
        for w in writes:
            self.D(w).did_wr(tok)
        return tok

    def dma(self, dst, src, reads=(), writes=(), stream="small", queue="sp"):
        p = self.p
        if self.serialize:
            for o in ENGS:
                p.need(queue, o, p.cnt[o])
            for st in p.DSTREAMS:
                p.need(queue, st, p.dmacnt[st])
        for r in reads:
            self.D(r).rd(p, queue)
        for w in writes:
            self.D(w).wr(p, queue)
        tok = p.dma(dst, src, stream, queue)
        for r in reads:
            self.D(r).did_rd(tok)
        for w in writes:
            self.D(w).did_wr(tok)
        return tok

    def tap(self, name, ap, shape, deps):
        if name not in self.taps:
            return
        t = self.nc.dram_tensor("tap_" + name, shape, ap.dtype,
                                kind="ExternalOutput").ap()
        self.tapnames.append("tap_" + name)
        self.dma(t, ap, reads=deps, writes=(), stream="tap")

    def declare(self):
        nc, es = self.nc, self.es
        self.hbm = {}

        def hin(name, shape):
            self.hbm[name] = nc.dram_tensor(name, shape, F32,
                                            kind="ExternalInput").ap()
        hin("x_t", [D, Lseq])
        self.out_h = nc.dram_tensor("out_t", [D, Lseq], F32,
                                    kind="ExternalOutput").ap()
        self.big = {}
        for nm, (K, Mm) in {"inA": (D, 2 * E), "inB": (D, 2 * E),
                            "poA": (E, D), "poB": (E, D),
                            "W1": (D, H), "W2": (H, D)}.items():
            hin(nm + "_lhsT", [K, Mm])
            kb, mb = K // 128, Mm // 128
            self.big[nm] = dict(K=K, M=Mm, KB=kb, MB=mb,
                                scr=nc.dram_tensor("scr_" + nm,
                                                   [128, kb * Mm], BF16).ap())
        for nm, sh in [("xpA_lhsT", [E, 96]), ("xpB_lhsT", [E, 96]),
                       ("dtpA_lhsT", [RK, E]), ("dtpB_lhsT", [RK, E]),
                       ("Alog_r", [128, 2 * EB * NS]), ("dtpb_r", [128, 2 * EB]),
                       ("cw_r", [128, 2 * EB * 4]), ("cb_r", [128, 2 * EB]),
                       ("Dsk_r", [128, 2 * EB]),
                       ("lnw_r", [128, 2 * DB]), ("lnb_r", [128, 2 * DB]),
                       ("nfw_r", [128, DB]), ("nfb_r", [128, DB]),
                       ("b1_r", [128, HB]), ("b2_r", [128, DB]),
                       ("sc4_r", [4, 1]), ("sc16_r", [16, 1]),
                       ("onesmean_r", [128, 1]), ("onesrow_r", [1, 128]),
                       ("epsc_r", [128, 1]), ("nhalf_r", [128, 1]),
                       ("delta16_r", [16, NS * 128]), ("ident_r", [128, 128]),
                       ("mask1_r", [4, HB]), ("mask2_r", [4, DB])]:
            hin(nm, sh)

        s = {}

        def sb(name, shape, dt):
            s[name] = es.enter_context(nc.sbuf_tensor(name, shape, dt))
        sb("x_res", [128, DB * Lc], F32)
        sb("t_str", [128, DB * Lc], F32)
        sb("xnf", [128, DB * Lc], F32)
        sb("xin_bf", [128, DB * Lc], BF16)
        sb("y_bf", [128, 2 * Lc], BF16)
        sb("xcin", [128, 2 * (Lc + 3)], F32)
        sb("tail3", [128, 2 * EB * 3], F32)
        sb("cconv", [128, 2 * Lc], F32)
        sb("sgtmp", [128, 2 * Lc], F32)
        sb("xc_bf", [128, EB * Lc], BF16)
        sb("zs_bf", [128, EB * Lc], BF16)
        sb("dtneg", [128, EB * Lc], BF16)
        sb("dtraw_bf", [64, Lc], BF16)
        sb("Bneg_bf", [16, Lc], BF16)
        sb("C_bf", [16, Lc], BF16)
        sb("B_bc", [128, NS * Lc], BF16)
        sb("C_bc", [128, NS * Lc], BF16)
        sb("u_bf", [128, 2 * Lc], BF16)
        sb("dA", [128, NS * Lc], F32)
        sb("dBu", [128, NS * Lc], BF16)
        sb("h_bf", [128, NS * Lc], BF16)
        sb("ytmp", [128, 2 * Lc], F32)
        sb("carry", [128, 2 * EB * NS], F32)
        sb("hg_bf", [128, HB * Lc], BF16)
        sb("wbuf0", [128, MGRP * 8 * 128], BF16)
        sb("wbuf1", [128, MGRP * 8 * 128], BF16)
        sb("xp_sb", [128, 2 * EB * 96], BF16)
        sb("dtp_sb", [64, 2 * EB * 128], BF16)
        sb("Aneg", [128, 2 * EB * NS], F32)
        sb("ndtpb", [128, 2 * EB], F32)
        sb("cwsb", [128, 2 * EB * 4], F32)
        sb("cbc", [128, 2 * EB], F32)
        sb("Dsk", [128, 2 * EB], F32)
        sb("lnw", [128, 2 * DB], F32)
        sb("lnb", [128, 2 * DB], F32)
        sb("nfw", [128, DB], F32)
        sb("nfb", [128, DB], F32)
        sb("b1c", [128, HB], F32)
        sb("b2c", [128, DB], F32)
        sb("onesm", [128, 1], BF16)
        sb("epsc", [128, 1], F32)
        sb("nhalf", [128, 1], F32)
        sb("onesr_f", [1, 128], F32)
        sb("d16", [16, NS * 128], BF16)
        sb("sc4", [4, 1], F32)
        sb("sc16v", [16, 1], F32)
        sb("murow", [1, Lc], F32)
        sb("m2row", [1, Lc], F32)
        sb("vrow", [1, Lc], F32)
        sb("rrow", [1, Lc], F32)
        sb("ident", [128, 128], BF16)
        sb("mask1", [4, HB], F32)
        sb("mask2", [4, DB], F32)
        sb("u_lt", [128, DB * 16], BF16)
        sb("T1sb", [4, D], BF16)
        sb("T2sb", [4, D], BF16)
        sb("coef_bf", [4, Lc], BF16)
        sb("zbt", [4, 8 * 128], BF16)
        sb("g4", [4, 4 * Lc], BF16)
        self.s = s

        raw = {}

        def psd(name, shape):
            raw[name] = es.enter_context(nc.psum_tensor(name, shape, F32))
        psd("ps_a", [128, Lc])
        psd("ps_b", [128, Lc])
        psd("ps_x", [16, 2 * Lc])
        psd("ps_y", [128, 2 * Lc])
        psd("ps_o", [128, DB * Lc])
        ps = {
            "ps_a": raw["ps_a"][:, :],
            "ps_b": raw["ps_b"][:, :],
            "ps_bm": raw["ps_x"][0:16, 0:Lc],
            "ps_cm": raw["ps_x"][0:16, Lc:2 * Lc],
            "ps_s1": raw["ps_x"][0:1, 0:Lc],
            "ps_s2": raw["ps_x"][0:1, Lc:2 * Lc],
            "ps_bc": raw["ps_y"][:, 0:Lc],
            "ps_dt": raw["ps_y"][0:64, Lc:2 * Lc],
            "ps_o": raw["ps_o"][:, :],
        }
        self.ps = ps
        for alias, owner in (("ps_bm", "ps_x"), ("ps_cm", "ps_x"),
                             ("ps_s1", "ps_x"), ("ps_s2", "ps_x"),
                             ("ps_bc", "ps_y"), ("ps_dt", "ps_y")):
            self.dep[alias] = self.D(owner)

        self.p.alloc_sems(es)
        self.blk = es.enter_context(nc.Block())

    def bc8(self, psname):
        assert psname == "ps_bc"
        t = self.ps[psname].tensor
        return bass.AP(t, 0, [[2 * Lc, 128], [0, DB], [1, Lc]])

    def st0(self, ap_tensor, offset, reps, inner):
        return bass.AP(ap_tensor, offset, [[Lc, 128], [0, reps], [1, inner]])

    def body(self):
        self.p.scope = "prep"
        self.prepped = set()
        self.prep()
        for c in range(self.nch):
            self.chunk(c)
        self.p.scope = None
        p = self.p
        for st in p.DSTREAMS:
            v = p.dmacnt[st]
            if v:
                p.steps["sp"].append(
                    lambda h, sem=p.sems[st], v=v: h.wait_ge(sem, v))

    def prep(self):
        s, hbm, p = self.s, self.hbm, self.p
        op, dma = self.op, self.dma

        for src, dst in [("cw_r", "cwsb"), ("cb_r", "cbc"), ("Dsk_r", "Dsk"),
                         ("lnw_r", "lnw"), ("lnb_r", "lnb"), ("nfw_r", "nfw"),
                         ("nfb_r", "nfb"), ("b1_r", "b1c"), ("b2_r", "b2c"),
                         ("sc4_r", "sc4"), ("sc16_r", "sc16v"),
                         ("onesrow_r", "onesr_f"), ("mask1_r", "mask1"),
                         ("epsc_r", "epsc"), ("nhalf_r", "nhalf"),
                         ("mask2_r", "mask2")]:
            dma(s[dst][:, :], hbm[src], writes=(dst,))
        stg = s["dA"]
        dma(stg[:, 0:2 * EB * NS], hbm["Alog_r"], writes=("dA",))
        op("act", lambda h: h.activation(s["Aneg"][:, :], stg[:, 0:2 * EB * NS],
                                         AF.Exp),
           reads=("dA",), writes=("Aneg",))
        dma(stg[:, 0:2 * EB], hbm["dtpb_r"], writes=("dA",))
        op("dve", lambda h: h.tensor_scalar_mul(s["ndtpb"][:, :],
                                                stg[:, 0:2 * EB], -1.0),
           reads=("dA",), writes=("ndtpb",))
        dma(stg[:, 0:1], hbm["onesmean_r"], writes=("dA",))
        op("dve", lambda h: h.tensor_copy(s["onesm"][:, 0:1], stg[:, 0:1]),
           reads=("dA",), writes=("onesm",))
        dma(stg[0:16, 0:NS * 128], hbm["delta16_r"], writes=("dA",))
        op("dve", lambda h: h.tensor_copy(s["d16"][:, :], stg[0:16, 0:NS * 128]),
           reads=("dA",), writes=("d16",))
        dma(stg[:, 0:128], hbm["ident_r"], writes=("dA",))
        op("dve", lambda h: h.tensor_copy(s["ident"][:, :], stg[:, 0:128]),
           reads=("dA",), writes=("ident",))
        for L_, nm in ((0, "xpA_lhsT"), (1, "xpB_lhsT")):
            for k in range(EB):
                src2 = bass.AP(hbm[nm].tensor, k * 128 * 96, [[96, 128], [1, 96]])
                dma(stg[:, 0:96], src2, writes=("dA",))
                op("act", lambda h, L_=L_, k=k: h.activation(
                    s["xp_sb"][:, (L_ * EB + k) * 96:(L_ * EB + k + 1) * 96],
                    stg[:, 0:96], AF.Copy),
                   reads=("dA",), writes=("xp_sb",))
        for L_, nm in ((0, "dtpA_lhsT"), (1, "dtpB_lhsT")):
            dma(stg[0:64, 0:E], bass.AP(hbm[nm].tensor, 0, [[E, 64], [1, E]]),
                writes=("dA",))
            op("act", lambda h, L_=L_: h.activation(
                s["dtp_sb"][0:64, L_ * EB * 128:(L_ + 1) * EB * 128],
                stg[0:64, 0:E], AF.Copy),
               reads=("dA",), writes=("dtp_sb",))
        op("dve", lambda h: h.memset(s["tail3"][:, :], 0.0), writes=("tail3",))
        self.prep_slot = 0
        for nm in ("inA", "poA", "inB", "poB", "W1", "W2"):
            self.prep_big(nm)
        base = Dep()
        for nm2 in ("dA", "pstg0", "pstg1"):
            d0 = self.D(nm2)
            for e, v in d0.w.items():
                base.w[e] = max(base.w.get(e, 0), v)
            for e, v in d0.r.items():
                base.r[e] = max(base.r.get(e, 0), v)
        for i in range(2 * NS):
            d = Dep()
            d.w = dict(base.w)
            d.r = dict(base.r)
            self.dep["dAn%d" % i] = d
        tgt = self.D("dBu")
        for nm2 in ("pcv0", "pcv1"):
            d0 = self.D(nm2)
            for e, v in d0.w.items():
                tgt.w[e] = max(tgt.w.get(e, 0), v)
            for e, v in d0.r.items():
                tgt.r[e] = max(tgt.r.get(e, 0), v)

    def prep_big(self, nm):
        if nm in self.prepped:
            return
        self.prepped.add(nm)
        s, hbm, p = self.s, self.hbm, self.p
        op, dma = self.op, self.dma
        stg = s["dA"]
        oldscope = p.scope
        p.scope = "prep"
        owners_a = ["dA"] + ["dA%d" % n for n in range(NS)]
        owners_b = ["dBu"]
        for sl in range(4):
            for nm2, owners in (("pstg%d" % sl, owners_a), ("pcv%d" % sl, owners_b)):
                d = self.D(nm2)
                for ow in owners:
                    o = self.D(ow)
                    for e, v in o.w.items():
                        d.w[e] = max(d.w.get(e, 0), v)
                    for e, v in o.r.items():
                        d.r[e] = max(d.r.get(e, 0), v)
        if True:
            info = self.big[nm]
            lhsT, scr = hbm[nm + "_lhsT"], info["scr"]
            KB, Mm = info["KB"], info["M"]
            kmajor = nm in ("poA", "poB")
            units = []
            if kmajor:
                for k in range(KB):
                    units.append((bass.AP(lhsT.tensor, k * 128 * Mm,
                                          [[Mm, 128], [1, Mm]]),
                                  k * Mm, Mm))
            else:
                mg = 32 // KB
                gw = mg * 128
                ngrp = info["MB"] // mg
                per = min(2048 // gw, ngrp)
                for gg in range(ngrp // per):
                    for k in range(KB):
                        units.append((bass.AP(lhsT.tensor,
                                              k * 128 * Mm + gg * per * gw,
                                              [[Mm, 128], [1, per * gw]]),
                                      None, (gg, k, per, gw)))
            if kmajor:
                gsz = 2 * Mm
            else:
                gsz = KB * (32 // KB) * 128
            ldstreams = ("wb0", "wb1", "small")
            ststreams = ("prep", "xin", "out")
            for src2, dsto, meta in units:
                sl = self.prep_slot % 2
                six = self.prep_slot % 3
                self.prep_slot += 1
                if dsto is not None:
                    w = meta
                    ldsl = stg[:, sl * 2048:sl * 2048 + w]
                    cvsl = s["dBu"][:, sl * 2048:sl * 2048 + w]
                    dma(ldsl, src2, writes=("pstg%d" % sl,),
                        stream=ldstreams[six], queue="sp")
                    op("dve", lambda h, ldsl=ldsl, cvsl=cvsl: h.tensor_copy(
                        cvsl, ldsl),
                       reads=("pstg%d" % sl,), writes=("pcv%d" % sl,))
                    dst = bass.AP(scr.tensor, dsto, [[KB * Mm, 128], [1, w]])
                    gdep = "scr_%s_%d" % (nm, dsto // gsz)
                    dma(dst, cvsl, reads=("pcv%d" % sl,), writes=(gdep,),
                        stream=ststreams[six], queue="act")
                else:
                    gg, k, per, gw = meta
                    w = per * gw
                    ldsl = stg[:, sl * 2048:sl * 2048 + w]
                    cvsl = s["dBu"][:, sl * 2048:sl * 2048 + w]
                    dma(ldsl, src2, writes=("pstg%d" % sl,),
                        stream=ldstreams[six], queue="sp")
                    op("dve", lambda h, ldsl=ldsl, cvsl=cvsl: h.tensor_copy(
                        cvsl, ldsl),
                       reads=("pstg%d" % sl,), writes=("pcv%d" % sl,))
                    grp0 = gg * per
                    dst = bass.AP(scr.tensor,
                                  grp0 * (KB * gw) + k * gw,
                                  [[KB * Mm, 128], [KB * gw, per], [1, gw]])
                    gdeps = tuple("scr_%s_%d" % (nm, grp0 + j) for j in range(per))
                    dma(dst, cvsl, reads=("pcv%d" % sl,), writes=gdeps,
                        stream=ststreams[six], queue="act")
        for sl in range(4):
            for nm2 in ("pstg%d" % sl, "pcv%d" % sl):
                d = self.D(nm2)
                for tok_eng, tok_v in list(d.w.items()) + list(d.r.items()):
                    for ow in ["dA", "dBu"] + ["dA%d" % n for n in range(NS)]:
                        self.D(ow).did_rd((tok_eng, tok_v))

    def ln_stats(self, src_name, bf_into="xin_bf"):
        s, ps, op = self.s, self.ps, self.op
        src = s[src_name]
        op("act", lambda h: h.activation(s[bf_into][:, :], src[:, :], AF.Copy),
           reads=(src_name,), writes=(bf_into,))
        op("act", lambda h: h.activation(s["hg_bf"][:, 0:DB * Lc], s[bf_into][:, :],
                                         AF.Square),
           reads=(bf_into,), writes=("hg_bf",))
        for k in range(DB):
            op("pe", lambda h, k=k: h.matmul(
                ps["ps_s1"][0:1, :], s["onesm"][:, 0:1],
                s[bf_into][:, k * Lc:(k + 1) * Lc],
                start=(k == 0), stop=(k == DB - 1)),
               reads=("onesm", bf_into),
               writes=("ps_s1",) if k == DB - 1 else (),
               wchk=("ps_s1",) if k == 0 else ())
        for k in range(DB):
            op("pe", lambda h, k=k: h.matmul(
                ps["ps_s2"][0:1, :], s["onesm"][:, 0:1],
                s["hg_bf"][:, k * Lc:(k + 1) * Lc],
                start=(k == 0), stop=(k == DB - 1)),
               reads=("onesm", "hg_bf"),
               writes=("ps_s2",) if k == DB - 1 else (),
               wchk=("ps_s2",) if k == 0 else ())
        op("dve", lambda h: h.tensor_copy(s["murow"][0:1, :], ps["ps_s1"][0:1, :]),
           reads=("ps_s1",), writes=("murow",))
        op("dve", lambda h: h.tensor_mul(s["vrow"][0:1, :], s["murow"][0:1, :],
                                         s["murow"][0:1, :]),
           reads=("murow",), writes=("vrow",))
        op("dve", lambda h: h.tensor_sub(s["vrow"][0:1, :], ps["ps_s2"][0:1, :],
                                         s["vrow"][0:1, :]),
           reads=("ps_s2", "vrow"), writes=("vrow",))
        op("act", lambda h: h.activation(s["vrow"][0:1, :], s["vrow"][0:1, :],
                                         AF.Ln, bias=s["epsc"][0:1, 0:1]),
           reads=("vrow", "epsc"), writes=("vrow",))
        op("act", lambda h: h.activation(s["rrow"][0:1, :], s["vrow"][0:1, :],
                                         AF.Exp, scale=s["nhalf"][0:1, 0:1]),
           reads=("vrow", "nhalf"), writes=("rrow",))

    def ln_bcast(self, rowname, psname="ps_bc"):
        s, ps, op = self.s, self.ps, self.op
        op("pe", lambda h: h.matmul(ps[psname][:, :], s["onesr_f"][0:1, :],
                                    s[rowname][0:1, :], start=True, stop=True),
           reads=("onesr_f", rowname), writes=(psname,), wchk=(psname,))

    def ln_apply(self, src_name, out_name, affine=None, out_dt_bf=True):
        s, op = self.s, self.op
        src, out = s[src_name], s[out_name]
        self.ln_bcast("murow")
        mu = self.bc8("ps_bc")
        op("dve", lambda h: h.tensor_sub(s["xnf"][:, :], src[:, :], mu),
           reads=(src_name, "ps_bc"), writes=("xnf",))
        self.ln_bcast("rrow")
        r = self.bc8("ps_bc")
        if affine is None:
            op("dve", lambda h: h.tensor_mul(out[:, :], s["xnf"][:, :], r),
               reads=("xnf", "ps_bc"), writes=(out_name,))
        else:
            wnm, bnm, woff = affine
            op("dve", lambda h: h.tensor_mul(s["xnf"][:, :], s["xnf"][:, :], r),
               reads=("xnf", "ps_bc"), writes=("xnf",))
            wap = self.st0w(s[wnm], woff)
            bap = self.st0w(s[bnm], woff)
            op("dve", lambda h: h.scalar_tensor_tensor(
                out[:, :], s["xnf"][:, :], 1.0, s["xnf"][:, :], ALU.mult,
                ALU.add) if False else h.tensor_scalar(
                out[:, :], s["xnf"][:, :], wap, bap, ALU.mult, ALU.add),
               reads=("xnf", wnm, bnm), writes=(out_name,))

    def st0w(self, tile, off):
        raise RuntimeError("unused")

    def chunk(self, c):
        s, p = self.s, self.p
        op, dma = self.op, self.dma
        src = bass.AP(self.hbm["x_t"].tensor, c * Lc,
                      [[Lseq, 128], [128 * Lseq, DB], [1, Lc]])
        dma(s["x_res"][:, :], src, writes=("x_res",), stream="xin")
        self.p.scope = "ln0"
        self.ln_stats("x_res")
        self.ln_apply("x_res", "t_str")
        for L_ in range(2):
            self.mamba_layer(c, L_)
        self.tap("c%d_t2" % c, s["t_str"][:, :], [128, DB * Lc], ("t_str",))
        self.p.scope = "normf"
        self.ln_stats("t_str")
        self.ln_bcast("murow")
        op("dve", lambda h: h.tensor_sub(s["xnf"][:, :], s["t_str"][:, :],
                                         self.bc8("ps_bc")),
           reads=("t_str", "ps_bc"), writes=("xnf",))
        self.ln_bcast("rrow")
        op("dve", lambda h: h.tensor_mul(s["xnf"][:, :], s["xnf"][:, :],
                                         self.bc8("ps_bc")),
           reads=("xnf", "ps_bc"), writes=("xnf",))
        for k in range(DB):
            op("dve", lambda h, k=k: h.tensor_scalar(
                s["xnf"][:, k * Lc:(k + 1) * Lc], s["xnf"][:, k * Lc:(k + 1) * Lc],
                s["nfw"][:, k:k + 1], s["nfb"][:, k:k + 1],
                ALU.mult, ALU.add),
               reads=("xnf", "nfw", "nfb"), writes=("xnf",))
        op("dve", lambda h: h.tensor_add(s["x_res"][:, :], s["x_res"][:, :],
                                         s["xnf"][:, :]),
           reads=("x_res", "xnf"), writes=("x_res",))
        self.ln_stats("x_res")
        self.ln_bcast("murow")
        op("dve", lambda h: h.tensor_sub(s["xnf"][:, :], s["x_res"][:, :],
                                         self.bc8("ps_bc")),
           reads=("x_res", "ps_bc"), writes=("xnf",))
        self.ln_bcast("rrow")
        op("dve", lambda h: h.tensor_mul(s["xin_bf"][:, :], s["xnf"][:, :],
                                         self.bc8("ps_bc")),
           reads=("xnf", "ps_bc"), writes=("xin_bf",))
        if c == 0:
            self.p.scope = "loraprep"
            self.lora_prep()
        self.p.scope = "ffn"
        self.ffn(c)
        dst = bass.AP(self.out_h.tensor, c * Lc,
                      [[Lseq, 128], [128 * Lseq, DB], [1, Lc]])
        dma(dst, s["x_res"][:, :], reads=("x_res",), stream="out")

    def stream_groups(self, nm):
        info = self.big[nm]
        KB = info["KB"]
        mg = 32 // KB
        for grp in range(info["MB"] // mg):
            bufn = "wbuf%d" % (grp % 2)
            buf = self.s[bufn]
            src = bass.AP(info["scr"].tensor, grp * (KB * mg * 128),
                          [[KB * info["M"], 128], [1, KB * mg * 128]])
            self.dma(buf[:, 0:KB * mg * 128], src,
                     reads=("scr_%s_%d" % (nm, grp),), writes=(bufn,),
                     stream="wb0" if grp % 2 == 0 else "wb1")

            def tile(k, mi, buf=buf, mg=mg):
                off = k * (mg * 128) + mi * 128
                return buf[:, off:off + 128]
            yield grp, mg, bufn, tile

    def stream_kgroups(self, nm, kg=2):
        info = self.big[nm]
        KB, MB = info["KB"], info["MB"]
        w = kg * MB * 128
        for grp in range(KB // kg):
            bufn = "wbuf%d" % (grp % 2)
            buf = self.s[bufn]
            src = bass.AP(info["scr"].tensor, grp * w,
                          [[KB * info["M"], 128], [1, w]])
            self.dma(buf[:, 0:w], src, reads=("scr_%s_%d" % (nm, grp),),
                     writes=(bufn,),
                     stream="wb0" if grp % 2 == 0 else "wb1")

            def tile(kig, m, buf=buf, MB=MB):
                off = kig * (MB * 128) + m * 128
                return buf[:, off:off + 128]
            yield grp, bufn, tile

    def mamba_layer(self, c, L_):
        s, ps, p = self.s, self.ps, self.p
        op = self.op
        nm_in = "inA" if L_ == 0 else "inB"
        nm_po = "poA" if L_ == 0 else "poB"
        lnoff = L_ * DB
        eoff = L_ * EB

        self.p.scope = "lnin"
        self.ln_stats("t_str")
        self.ln_bcast("murow")
        op("dve", lambda h: h.tensor_sub(s["xnf"][:, :], s["t_str"][:, :],
                                         self.bc8("ps_bc")),
           reads=("t_str", "ps_bc"), writes=("xnf",))
        self.ln_bcast("rrow")
        op("dve", lambda h: h.tensor_mul(s["xnf"][:, :], s["xnf"][:, :],
                                         self.bc8("ps_bc")),
           reads=("xnf", "ps_bc"), writes=("xnf",))
        for k in range(DB):
            op("dve", lambda h, k=k: h.tensor_scalar(
                s["xin_bf"][:, k * Lc:(k + 1) * Lc],
                s["xnf"][:, k * Lc:(k + 1) * Lc],
                s["lnw"][:, lnoff + k:lnoff + k + 1],
                s["lnb"][:, lnoff + k:lnoff + k + 1],
                ALU.mult, ALU.add),
               reads=("xnf", "lnw", "lnb"), writes=("xin_bf",))

        self.tap("c%dL%d_tailpre" % (c, L_), s["tail3"][:, :],
                 [128, 2 * EB * 3], ("tail3",))
        self.p.scope = "inproj"
        for grp, mg, bufn, tile in self.stream_groups(nm_in):
            for mi in range(mg):
                m = grp * mg + mi
                bank = "ps_a" if m % 2 == 0 else "ps_b"
                for k in range(DB):
                    op("pe", lambda h, k=k, mi=mi, bank=bank, tile=tile: h.matmul(
                        ps[bank][:, :], tile(k, mi),
                        s["xin_bf"][:, k * Lc:(k + 1) * Lc],
                        start=(k == 0), stop=(k == DB - 1)),
                       reads=(bufn, "xin_bf"),
                       writes=(bank,) if k == DB - 1 else (),
                       wchk=(bank,) if k == 0 else ())
                self.inproj_evac(c, L_, m, bank, eoff)

        self.tap("c%dL%d_xin" % (c, L_), s["xin_bf"][:, :], [128, DB * Lc],
                 ("xin_bf",))
        self.p.scope = "xproj"
        for k in range(EB):
            xoff = (L_ * EB + k) * 96
            for tgt, dep, c0, c1 in ((ps["ps_a"][0:64, :], "ps_a", 0, 64),
                                     (ps["ps_bm"][0:16, :], "ps_x", 64, 80),
                                     (ps["ps_b"][0:16, :], "ps_b", 80, 96)):
                op("pe", lambda h, k=k, xoff=xoff, tgt=tgt, c0=c0, c1=c1: h.matmul(
                    tgt, s["xp_sb"][:, xoff + c0:xoff + c1],
                    s["xc_bf"][:, k * Lc:(k + 1) * Lc],
                    start=(k == 0), stop=(k == EB - 1)),
                   reads=("xp_sb", "xc_bf"),
                   writes=(dep,) if k == EB - 1 else (),
                   wchk=(dep,) if k == 0 else ())
        op("act", lambda h: h.activation(s["dtraw_bf"][0:64, :], ps["ps_a"][0:64, :],
                                         AF.Copy),
           reads=("ps_a",), writes=("dtraw_bf",))
        op("dve", lambda h: h.tensor_scalar_mul(s["Bneg_bf"][0:16, :],
                                                ps["ps_bm"][0:16, :], -1.0),
           reads=("ps_x",), writes=("Bneg_bf",))
        op("act", lambda h: h.activation(s["C_bf"][0:16, :], ps["ps_b"][0:16, :],
                                         AF.Copy),
           reads=("ps_b",), writes=("C_bf",))
        self.p.scope = "bcbuild"
        for n in range(NS):
            op("pe", lambda h, n=n: h.matmul(
                ps["ps_bc"][:, :], s["d16"][:, n * 128:(n + 1) * 128],
                s["Bneg_bf"][0:16, :], start=True, stop=True),
               reads=("d16", "Bneg_bf"), writes=("ps_bc",), wchk=("ps_bc",))
            op("act", lambda h, n=n: h.activation(
                s["B_bc"][:, n * Lc:(n + 1) * Lc], ps["ps_bc"][:, :], AF.Copy),
               reads=("ps_bc",), writes=("B_bc",))
            op("pe", lambda h, n=n: h.matmul(
                ps["ps_bc"][:, :], s["d16"][:, n * 128:(n + 1) * 128],
                s["C_bf"][0:16, :], start=True, stop=True),
               reads=("d16", "C_bf"), writes=("ps_bc",), wchk=("ps_bc",))
            op("act", lambda h, n=n: h.activation(
                s["C_bc"][:, n * Lc:(n + 1) * Lc], ps["ps_bc"][:, :], AF.Copy),
               reads=("ps_bc",), writes=("C_bc",))

        self.tap("c%dL%d_xc" % (c, L_), s["xc_bf"][:, :], [128, EB * Lc],
                 ("xc_bf",))
        self.tap("c%dL%d_zs" % (c, L_), s["zs_bf"][:, :], [128, EB * Lc],
                 ("zs_bf",))
        self.tap("c%dL%d_Bbc" % (c, L_), s["B_bc"][:, :], [128, NS * Lc],
                 ("B_bc",))
        self.tap("c%dL%d_Cbc" % (c, L_), s["C_bc"][:, :], [128, NS * Lc],
                 ("C_bc",))
        self.p.scope = "dtproj"
        for m in range(EB):
            bank = "ps_a" if m % 2 == 0 else "ps_b"
            op("pe", lambda h, m=m, bank=bank: h.matmul(
                ps[bank][:, :], s["dtp_sb"][0:64, (L_ * EB + m) * 128:
                                            (L_ * EB + m + 1) * 128],
                s["dtraw_bf"][0:64, :], start=True, stop=True),
               reads=("dtp_sb", "dtraw_bf"), writes=(bank,), wchk=(bank,))
            op("act", lambda h, m=m, bank=bank: h.activation(
                s["dtneg"][:, m * Lc:(m + 1) * Lc], ps[bank][:, :], AF.Sigmoid,
                scale=-1.0, bias=s["ndtpb"][:, eoff + m:eoff + m + 1]),
               reads=(bank, "ndtpb"), writes=("dtneg",))
        op("act", lambda h: h.activation(s["dtneg"][:, :], s["dtneg"][:, :], AF.Ln),
           reads=("dtneg",), writes=("dtneg",))

        self.tap("c%dL%d_dtneg" % (c, L_), s["dtneg"][:, :], [128, EB * Lc],
                 ("dtneg",))
        self.p.scope = "scan"
        po_iter = self.stream_kgroups(nm_po, kg=2)
        po_cur = None
        for blk in range(EB):
            self.scan_block(c, L_, blk, eoff)
            kig = blk % 2
            if kig == 0:
                po_cur = next(po_iter)
            grp, bufn, tile = po_cur
            for m in range(DB):
                op("pe", lambda h, kig=kig, m=m, tile=tile, blk=blk: h.matmul(
                    ps["ps_o"][:, m * Lc:(m + 1) * Lc], tile(kig, m),
                    s["y_bf"][:, (blk % 2) * Lc:(blk % 2 + 1) * Lc],
                    start=(blk == 0 and m % 2 == 0),
                    stop=(blk == EB - 1 and m % 2 == 1)),
                   reads=(bufn, "y%d" % (blk % 2)),
                   writes=("ps_o",) if (blk == EB - 1 and m == DB - 1) else (),
                   wchk=("ps_o",) if (blk == 0 and m == 0) else ())
        self.tap("c%dL%d_carry" % (c, L_), s["carry"][:, :], [128, 2 * EB * NS],
                 ("carry",))

        self.p.scope = "outproj"
        for m in range(DB):
            op("dve", lambda h, m=m: h.tensor_add(
                s["t_str"][:, m * Lc:(m + 1) * Lc],
                s["t_str"][:, m * Lc:(m + 1) * Lc],
                ps["ps_o"][:, m * Lc:(m + 1) * Lc]),
               reads=("t_str", "ps_o"), writes=("t_str",))

    def inproj_evac(self, c, L_, m, bank, eoff):
        s, ps, op = self.s, self.ps, self.op
        if m < EB:
            blk = m
            rot = blk % 2
            xoff = rot * (Lc + 3)
            op("dve", lambda h, blk=blk, xoff=xoff: h.tensor_copy(
                s["xcin"][:, xoff:xoff + 3],
                s["tail3"][:, (L_ * EB + blk) * 3:(L_ * EB + blk + 1) * 3]),
               reads=("tail3",), writes=("xcin%d" % rot,))
            op("dve", lambda h, xoff=xoff, bank=bank: h.tensor_copy(
                s["xcin"][:, xoff + 3:xoff + 3 + Lc], ps[bank][:, :]),
               reads=(bank,), writes=("xcin%d" % rot,))
            op("dve", lambda h, blk=blk, bank=bank: h.tensor_copy(
                s["tail3"][:, (L_ * EB + blk) * 3:(L_ * EB + blk + 1) * 3],
                ps[bank][:, Lc - 3:Lc]),
               reads=(bank,), writes=("tail3",))
            cvo = (L_ * EB + blk) * 4
            op("dve", lambda h, xoff=xoff, rot=rot, cvo=cvo, blk=blk: h.tensor_scalar(
                s["cconv"][:, rot * Lc:(rot + 1) * Lc],
                s["xcin"][:, xoff:xoff + Lc],
                s["cwsb"][:, cvo:cvo + 1], s["cbc"][:, eoff + blk:eoff + blk + 1],
                ALU.mult, ALU.add),
               reads=("xcin%d" % rot, "cwsb", "cbc"), writes=("cconv%d" % rot,))
            for kk in range(1, 4):
                op("dve", lambda h, xoff=xoff, rot=rot, cvo=cvo, kk=kk:
                   h.scalar_tensor_tensor(
                       s["cconv"][:, rot * Lc:(rot + 1) * Lc],
                       s["xcin"][:, xoff + kk:xoff + kk + Lc],
                       s["cwsb"][:, cvo + kk:cvo + kk + 1],
                       s["cconv"][:, rot * Lc:(rot + 1) * Lc],
                       ALU.mult, ALU.add),
                   reads=("xcin%d" % rot, "cwsb", "cconv%d" % rot),
                   writes=("cconv%d" % rot,))
            op("act", lambda h, rot=rot: h.activation(
                s["sgtmp"][:, rot * Lc:(rot + 1) * Lc],
                s["cconv"][:, rot * Lc:(rot + 1) * Lc], AF.Sigmoid),
               reads=("cconv%d" % rot,), writes=("sgtmp%d" % rot,))
            op("dve", lambda h, rot=rot, blk=blk: h.tensor_mul(
                s["xc_bf"][:, blk * Lc:(blk + 1) * Lc],
                s["cconv"][:, rot * Lc:(rot + 1) * Lc],
                s["sgtmp"][:, rot * Lc:(rot + 1) * Lc]),
               reads=("cconv%d" % rot, "sgtmp%d" % rot), writes=("xc_bf",))
        else:
            blk = m - EB
            rot = blk % 2
            op("act", lambda h, rot=rot, bank=bank: h.activation(
                s["sgtmp"][:, rot * Lc:(rot + 1) * Lc], ps[bank][:, :], AF.Sigmoid),
               reads=(bank,), writes=("sgtmp%d" % rot,))
            op("dve", lambda h, rot=rot, bank=bank, blk=blk: h.tensor_mul(
                s["zs_bf"][:, blk * Lc:(blk + 1) * Lc], ps[bank][:, :],
                s["sgtmp"][:, rot * Lc:(rot + 1) * Lc]),
               reads=(bank, "sgtmp%d" % rot), writes=("zs_bf",))

    def scan_block(self, c, L_, blk, eoff):
        s, op = self.s, self.op
        rot = blk % 2
        aoff = (L_ * EB + blk) * NS
        dAfull = s["dA"][:, :].bitcast(BF16)
        dA = dAfull[:, rot * NS * Lc:(rot + 1) * NS * Lc]
        for n in range(NS):
            op("act", lambda h, n=n, blk=blk, aoff=aoff, dA=dA: h.activation(
                dA[:, n * Lc:(n + 1) * Lc],
                s["dtneg"][:, blk * Lc:(blk + 1) * Lc], AF.Exp,
                scale=s["Aneg"][:, aoff + n:aoff + n + 1]),
               reads=("dtneg", "Aneg"), writes=("dAn%d" % (rot * NS + n),))
        op("dve", lambda h, blk=blk, rot=rot: h.tensor_mul(
            s["u_bf"][:, rot * Lc:(rot + 1) * Lc],
            s["dtneg"][:, blk * Lc:(blk + 1) * Lc],
            s["xc_bf"][:, blk * Lc:(blk + 1) * Lc]),
           reads=("dtneg", "xc_bf"), writes=("u%d" % rot,))
        u0 = bass.AP(s["u_bf"][:, :].tensor, rot * Lc,
                     [[2 * Lc, 128], [0, NS], [1, Lc]])
        import os as _os2
        dbu_eng = "pool" if _os2.environ.get("DBU_POOL", "0") == "1" else "dve"
        op(dbu_eng, lambda h, u0=u0: h.tensor_mul(s["dBu"][:, :], u0, s["B_bc"][:, :]),
           reads=("u%d" % rot, "B_bc"), writes=("dBu",))
        for n in range(NS):
            if c == 0:
                init = 0.0
            else:
                init = s["carry"][:, (aoff + n):(aoff + n) + 1]
            op("dve", lambda h, n=n, init=init, dA=dA: h.tensor_tensor_scan(
                s["h_bf"][:, n * Lc:(n + 1) * Lc],
                dA[:, n * Lc:(n + 1) * Lc],
                s["dBu"][:, n * Lc:(n + 1) * Lc],
                init, ALU.mult, ALU.add),
               reads=("dAn%d" % (rot * NS + n), "dBu") + (("carry",) if c > 0 else ()),
               writes=("h_bf",))
        op("dve", lambda h: h.tensor_mul(s["dBu"][:, :], s["h_bf"][:, :],
                                         s["C_bc"][:, :]),
           reads=("h_bf", "C_bc"), writes=("dBu",))
        w_ = NS
        lvl = 0
        while w_ > 1:
            half = w_ // 2
            for j in range(half):
                op("dve", lambda h, j=j: h.tensor_add(
                    s["dBu"][:, j * Lc:(j + 1) * Lc],
                    s["dBu"][:, (2 * j) * Lc:(2 * j + 1) * Lc],
                    s["dBu"][:, (2 * j + 1) * Lc:(2 * j + 2) * Lc]),
                   reads=("dBu",), writes=("dBu",))
            if lvl == 0:
                hlast = bass.AP(s["h_bf"][:, :].tensor, Lc - 1,
                                [[NS * Lc, 128], [Lc, NS]])
                op("dve", lambda h, hlast=hlast, aoff=aoff: h.tensor_copy(
                    s["carry"][:, aoff:aoff + NS], hlast),
                   reads=("h_bf",), writes=("carry",))
            w_ = half
            lvl += 1
        op("dve", lambda h, rot=rot, blk=blk: h.scalar_tensor_tensor(
            s["ytmp"][:, rot * Lc:(rot + 1) * Lc],
            s["xc_bf"][:, blk * Lc:(blk + 1) * Lc],
            s["Dsk"][:, eoff + blk:eoff + blk + 1],
            s["dBu"][:, 0:Lc], ALU.mult, ALU.add),
           reads=("xc_bf", "Dsk", "dBu"), writes=("yt%d" % rot,))
        op("dve", lambda h, rot=rot, blk=blk: h.tensor_mul(
            s["y_bf"][:, rot * Lc:(rot + 1) * Lc],
            s["ytmp"][:, rot * Lc:(rot + 1) * Lc],
            s["zs_bf"][:, blk * Lc:(blk + 1) * Lc]),
           reads=("yt%d" % rot, "zs_bf"), writes=("y%d" % rot,))

    def lora_prep(self):
        s, ps, op = self.s, self.ps, self.op
        src = bass.AP(s["xin_bf"][:, :].tensor, 1, [[DB * Lc, 128], [Lc, DB], [1, 16]])
        op("dve", lambda h, src=src: h.tensor_copy(s["u_lt"][:, :], src),
           reads=("xin_bf",), writes=("u_lt",))
        for tgt, base in (("T1sb", 0), ("T2sb", 8)):
            for blk in range(DB):
                inap = bass.AP(s["u_lt"][:, :].tensor, blk * 16 + base,
                               [[DB * 16, 128], [2, 4]])
                trout = ps["ps_bm"][0:4, 0:64].bitcast(BF16)
                op("pe", lambda h, inap=inap, trout=trout: h.transpose(
                    trout, inap, s["ident"][:, :]),
                   reads=("u_lt", "ident"), writes=("ps_bm",), wchk=("ps_bm",))
                op("act", lambda h, tgt=tgt, blk=blk, trout=trout: h.activation(
                    s[tgt][0:4, blk * 128:(blk + 1) * 128], trout, AF.Copy),
                   reads=("ps_bm",), writes=(tgt,))


    def ffn(self, c):
        s, ps, op = self.s, self.ps, self.op
        for k in range(DB):
            st = bass.AP(s["u_lt"][:, :].tensor, k * 16 + 1, [[DB * 16, 128], [2, 4]])
            op("pe", lambda h, k=k, st=st: h.matmul(
                ps["ps_bm"][0:4, :], st, s["xin_bf"][:, k * Lc:(k + 1) * Lc],
                start=(k == 0), stop=(k == DB - 1)),
               reads=("u_lt", "xin_bf"), writes=("ps_bm",) if k == DB - 1 else (),
               wchk=("ps_bm",) if k == 0 else ())
        op("dve", lambda h: h.tensor_scalar_mul(s["coef_bf"][0:4, :],
                                                ps["ps_bm"][0:4, :],
                                                s["sc4"][0:4, 0:1]),
           reads=("ps_bm", "sc4"), writes=("coef_bf",))
        for grp, mg, bufn, tile in self.stream_groups("W1"):
            for mi in range(mg):
                m = grp * mg + mi
                bank = "ps_a" if m % 2 == 0 else "ps_b"
                for k in range(DB):
                    op("pe", lambda h, k=k, mi=mi, bank=bank, tile=tile: h.matmul(
                        ps[bank][:, :], tile(k, mi),
                        s["xin_bf"][:, k * Lc:(k + 1) * Lc],
                        start=(k == 0), stop=False),
                       reads=(bufn, "xin_bf"),
                       wchk=(bank,) if k == 0 else ())
                zr = (m % 2) * 512
                op("dve", lambda h, m=m, zr=zr: h.tensor_scalar_mul(
                    s["zbt"][0:4, zr:zr + 128],
                    s["T1sb"][0:4, (m % 8) * 128:(m % 8 + 1) * 128],
                    s["mask1"][0:4, m:m + 1]),
                   reads=("T1sb", "mask1"), writes=("zbt%d" % (m % 2),))
                op("pe", lambda h, m=m, bank=bank, zr=zr: h.matmul(
                    ps[bank][:, :], s["zbt"][0:4, zr:zr + 128],
                    s["coef_bf"][0:4, :], start=False, stop=True),
                   reads=("zbt%d" % (m % 2), "coef_bf"), writes=(bank,))
                op("act", lambda h, m=m, bank=bank: h.activation(
                    s["hg_bf"][:, m * Lc:(m + 1) * Lc], ps[bank][:, :], AF.Gelu,
                    bias=s["b1c"][:, m:m + 1]),
                   reads=(bank, "b1c"), writes=("hg_bf",))
        gbanks = ("ps_bm", "ps_cm", "ps_bc", "ps_dt")
        for sp_ in range(4):
            psn = gbanks[sp_]
            for i, kb in enumerate(range(sp_ * 8, sp_ * 8 + 8)):
                st = bass.AP(s["u_lt"][:, :].tensor, (kb % 8) * 16 + 9,
                             [[DB * 16, 128], [2, 4]])
                op("pe", lambda h, kb=kb, st=st, psn=psn, i=i: h.matmul(
                    self.ps[psn][0:4, :],
                    st, s["hg_bf"][:, kb * Lc:(kb + 1) * Lc],
                    start=(i == 0), stop=(i == 7)),
                   reads=("u_lt", "hg_bf"), writes=(psn,) if i == 7 else (),
                   wchk=(psn,) if i == 0 else ())
            op("dve", lambda h, sp_=sp_, psn=psn: h.tensor_scalar_mul(
                s["g4"][0:4, sp_ * Lc:(sp_ + 1) * Lc],
                self.ps[psn][0:4, :],
                s["sc4"][0:4, 0:1]),
               reads=(psn, "sc4"), writes=("g4",))
        for grp, mg, bufn, tile in self.stream_groups("W2"):
            for mi in range(mg):
                m = grp * mg + mi
                bank = "ps_a" if m % 2 == 0 else "ps_b"
                for k in range(HB):
                    op("pe", lambda h, k=k, mi=mi, bank=bank, tile=tile: h.matmul(
                        ps[bank][:, :], tile(k, mi),
                        s["hg_bf"][:, k * Lc:(k + 1) * Lc],
                        start=(k == 0), stop=False),
                       reads=(bufn, "hg_bf"),
                       wchk=(bank,) if k == 0 else ())
                zr = (m % 2) * 512
                for sp_ in range(4):
                    gat = bass.AP(s["T2sb"][:, :].tensor, (m % 2) * 512 + sp_,
                                  [[D, 4], [4, 128]])
                    op("dve", lambda h, m=m, sp_=sp_, gat=gat, zr=zr:
                       h.tensor_scalar_mul(
                           s["zbt"][0:4, zr + sp_ * 128:zr + (sp_ + 1) * 128],
                           gat, s["mask2"][0:4, m:m + 1]),
                       reads=("T2sb", "mask2"), writes=("zbt%d" % (m % 2),))
                for sp_ in range(4):
                    op("pe", lambda h, m=m, sp_=sp_, bank=bank, zr=zr: h.matmul(
                        ps[bank][:, :],
                        s["zbt"][0:4, zr + sp_ * 128:zr + (sp_ + 1) * 128],
                        s["g4"][0:4, sp_ * Lc:(sp_ + 1) * Lc],
                        start=False, stop=(sp_ == 3)),
                       reads=("zbt%d" % (m % 2), "g4"),
                       writes=(bank,) if sp_ == 3 else ())
                op("dve", lambda h, m=m, bank=bank: h.scalar_tensor_tensor(
                    s["x_res"][:, m * Lc:(m + 1) * Lc], ps[bank][:, :],
                    s["b2c"][:, m:m + 1],
                    s["x_res"][:, m * Lc:(m + 1) * Lc], ALU.add, ALU.add),
                   reads=(bank, "b2c", "x_res"), writes=("x_res",))


def make_inputs(inp):
    f = np.float32
    A = np.ascontiguousarray
    outs = []
    const = {
        "sc4_r": np.repeat(inp["L0_ffn_scale"].reshape(1, 1), 4, 0).astype(f),
        "sc16_r": np.repeat(inp["L0_ffn_scale"].reshape(1, 1), 16, 0).astype(f),
        "onesmean_r": np.full((128, 1), 1.0 / D, f),
        "epsc_r": np.full((128, 1), EPS, f),
        "nhalf_r": np.full((128, 1), -0.5, f),
        "onesrow_r": np.ones((1, 128), f),
        "ident_r": np.eye(128, dtype=f),
    }
    d16 = np.zeros((16, NS * 128), f)
    for n in range(NS):
        d16[n, n * 128:(n + 1) * 128] = 1.0
    const["delta16_r"] = d16
    m1 = np.zeros((4, HB), f)
    for b2 in range(HB):
        m1[b2 // 8, b2] = 1.0
    const["mask1_r"] = m1
    m2 = np.zeros((4, DB), f)
    for b in range(DB):
        m2[b // 2, b] = 1.0
    const["mask2_r"] = m2

    def r3(x, nb):
        return A(x.reshape(2, nb, 128).transpose(2, 0, 1).reshape(128, 2 * nb))

    def r1(x, nb):
        return A(x.reshape(nb, 128).T)

    w = {
        "inA_lhsT": A(inp["L0_attn_in_proj_w"][0].T),
        "inB_lhsT": A(inp["L0_attn_in_proj_w"][1].T),
        "poA_lhsT": A(inp["L0_attn_out_proj_w"][0].T),
        "poB_lhsT": A(inp["L0_attn_out_proj_w"][1].T),
        "W1_lhsT": A(inp["L0_ffn_W1"].T),
        "W2_lhsT": A(inp["L0_ffn_W2"].T),
        "xpA_lhsT": A(inp["L0_attn_x_proj_w"][0].T),
        "xpB_lhsT": A(inp["L0_attn_x_proj_w"][1].T),
        "dtpA_lhsT": A(inp["L0_attn_dt_proj_w"][0].T),
        "dtpB_lhsT": A(inp["L0_attn_dt_proj_w"][1].T),
        "Alog_r": A(inp["L0_attn_A_log"].reshape(2, EB, 128, NS)
                    .transpose(2, 0, 1, 3).reshape(128, 2 * EB * NS)),
        "dtpb_r": r3(inp["L0_attn_dt_proj_b"], EB),
        "cw_r": A(inp["L0_attn_conv_w"].reshape(2, EB, 128, 4)
                  .transpose(2, 0, 1, 3).reshape(128, 2 * EB * 4)),
        "cb_r": r3(inp["L0_attn_conv_b"], EB),
        "Dsk_r": r3(inp["L0_attn_D_skip"], EB),
        "lnw_r": r3(inp["L0_attn_ln_w"], DB),
        "lnb_r": r3(inp["L0_attn_ln_b"], DB),
        "nfw_r": r1(inp["L0_attn_norm_f_w"], DB),
        "nfb_r": r1(inp["L0_attn_norm_f_b"], DB),
        "b1_r": r1(inp["L0_ffn_b1"], HB),
        "b2_r": r1(inp["L0_ffn_b2"], DB),
    }
    w.update(const)
    x = inp["L0_attn_x"]
    in_maps = []
    for b in range(x.shape[0]):
        m = dict(w)
        m["x_t"] = A(x[b].T)
        in_maps.append(m)
    return in_maps


def kernel(**inputs):
    in_maps = make_inputs(inputs)
    nc, _ = build()
    res = run_bass_kernel_spmd(nc, in_maps, core_ids=list(range(len(in_maps))))
    return np.stack([np.ascontiguousarray(r["out_t"].T) for r in res.results])
